# version 10
# baseline (speedup 1.0000x reference)
import contextlib
import numpy as np
import ml_dtypes
import concourse.bass as bass
import concourse.mybir as mybir
from concourse.bass_utils import run_bass_kernel_spmd

F32 = mybir.dt.float32
BF16 = mybir.dt.bfloat16
AF = mybir.ActivationFunctionType
ALU = mybir.AluOpType
AX = mybir.AxisListType

D = 2048
S = 2048
H = 8192
KC = 16
TB = 512
NTB = S // TB
ROT = D // 2
RC = 8
HC = 64
GC = 2 * HC
OC = 16
EPS = 1e-5
M32 = 12582912.0
M64 = 6755399441055744.0
QB = 127.0
ROPE_THETA = 10000.0
NCORES = 8


def _pi_perm():
    out = np.arange(D)
    for pos in range(ROT):
        c = pos // 128
        r = pos % 128
        out[pos] = 2 * (128 * (c // 2) + r) + (c % 2)
    return out


def _pair_tables():
    rot = np.float32(ROT)
    ar = np.arange(0, ROT, 2, dtype=np.float32)
    inv_freq = (np.float32(1.0) / np.float32(ROPE_THETA) ** (ar / rot)).astype(np.float32)
    tpos = np.arange(S, dtype=np.float32)
    freqs = (tpos[None, :] * inv_freq[:, None]).astype(np.float32)
    cosv = np.cos(freqs).astype(np.float32)
    sinv = np.sin(freqs).astype(np.float32)
    nb = RC // 2
    return (np.ascontiguousarray(cosv.reshape(nb, 128, S)),
            np.ascontiguousarray(sinv.reshape(nb, 128, S)))


def _img_tiles(mat_t):
    Dk, O = mat_t.shape
    nk, no = Dk // 128, O // 128
    a = mat_t.reshape(nk, 128, no, 128)
    return np.ascontiguousarray(a.transpose(2, 1, 0, 3).reshape(no, 128, nk * 128))


def build_host(inputs):
    X = inputs["L0_attn_X"]
    W_in = np.asarray(inputs["L0_attn_W_in"], np.float32)
    b_in = np.asarray(inputs["L0_attn_b_in"], np.float32)
    W_out = np.asarray(inputs["L0_attn_W_out"], np.float32)
    b_out = np.asarray(inputs["L0_attn_b_out"], np.float32)
    g = np.asarray(inputs["L0_ffn_norm_w"], np.float32)
    w_gv = np.asarray(inputs["L0_ffn_w_gv"], np.float32)
    w_o2 = np.asarray(inputs["L0_ffn_w_out"], np.float32)
    pi = _pi_perm()

    shared = {}
    Wqkv_t = np.concatenate([W_in[0:D][pi], W_in[D:2 * D][pi], W_in[2 * D:][pi]], axis=0).T
    shared["WQKV"] = _img_tiles(np.ascontiguousarray(Wqkv_t, dtype=np.float32))
    bqkv = np.concatenate([b_in[0:D][pi], b_in[D:2 * D][pi], b_in[2 * D:][pi]])
    shared["BIN"] = np.ascontiguousarray(bqkv.reshape(3 * KC, 128).T)
    shared["WOUT"] = _img_tiles(np.ascontiguousarray(W_out[:, pi].T, dtype=np.float32))
    shared["BOUT"] = np.ascontiguousarray(b_out.reshape(OC, 128).T)
    shared["G"] = np.ascontiguousarray(g.reshape(KC, 128).T)
    rho = np.empty(2 * H, dtype=np.int64)
    for c in range(HC):
        rho[256 * c:256 * c + 128] = np.arange(128 * c, 128 * (c + 1))
        rho[256 * c + 128:256 * c + 256] = H + np.arange(128 * c, 128 * (c + 1))
    shared["WGV"] = _img_tiles(np.ascontiguousarray(w_gv[rho].T, dtype=np.float32))
    shared["WO2"] = _img_tiles(np.ascontiguousarray(w_o2.T, dtype=np.float32))
    cos2, sin2 = _pair_tables()
    shared["COS"] = cos2
    shared["SIN"] = sin2
    shared["CONSTS"] = np.ascontiguousarray(
        np.broadcast_to(np.asarray([1.0, EPS, -0.5, 0.5], np.float32), (128, 4)))
    shared["ONESCOL"] = np.ones((128, 1), np.float32)
    shared["ONESROW"] = np.ones((1, 128), np.float32)
    shared["IDENT"] = np.eye(128, dtype=np.float32)

    per_core = []
    for b in range(NCORES):
        xb = np.asarray(X[b], np.float32).reshape(NTB, TB, KC, 128)
        img = np.ascontiguousarray(xb.transpose(0, 3, 2, 1).reshape(NTB, 128, KC * TB))
        m = {"XT": img}
        m.update(shared)
        per_core.append(m)
    return per_core


class _Sem:
    __slots__ = ("h", "n")

    def __init__(self, h):
        self.h = h
        self.n = 0


class _Prog:
    ENGS = ("tensor", "vector", "scalar", "sync", "gpsimd")

    def __init__(self):
        self.q = {e: [] for e in self.ENGS}

    def emit(self, eng, fn):
        self.q[eng].append(fn)

    def wait(self, eng, sem, n=None):
        if n is None:
            n = sem.n
        assert 0 <= n <= sem.n, f"wait {n} > scheduled {sem.n}"
        if n > 0:
            self.emit(eng, lambda e, h=sem.h, n=n: e.wait_ge(h, n))

    def run(self, eng, fn, inc=None, inc_n=1):
        if inc is not None:
            inc.n += inc_n

            def go(e, fn=fn, h=inc.h, k=inc_n):
                fn(e).then_inc(h, k)
        else:
            def go(e, fn=fn):
                fn(e)
        self.emit(eng, go)


def _rep0(base_ap, off, reps, width):
    pstep = base_ap.ap[0][0]
    return bass.AP(base_ap.tensor, base_ap.offset + off, [[pstep, 128], [0, reps], [1, width]])


def build_nc(debug=False):
    nc = bass.Bass("TRN2", target_bir_lowering=False, debug=False,
                   detect_race_conditions=False)
    XT = nc.dram_tensor("XT", [NTB, 128, KC * TB], F32, kind="ExternalInput").ap()
    WQKV = nc.dram_tensor("WQKV", [3 * KC, 128, KC * 128], F32, kind="ExternalInput").ap()
    BIN = nc.dram_tensor("BIN", [128, 3 * KC], F32, kind="ExternalInput").ap()
    WOUT = nc.dram_tensor("WOUT", [OC, 128, KC * 128], F32, kind="ExternalInput").ap()
    BOUT = nc.dram_tensor("BOUT", [128, OC], F32, kind="ExternalInput").ap()
    G = nc.dram_tensor("G", [128, KC], F32, kind="ExternalInput").ap()
    WGV = nc.dram_tensor("WGV", [GC, 128, KC * 128], F32, kind="ExternalInput").ap()
    WO2 = nc.dram_tensor("WO2", [OC, 128, HC * 128], F32, kind="ExternalInput").ap()
    COS = nc.dram_tensor("COS", [RC // 2, 128, S], F32, kind="ExternalInput").ap()
    SIN = nc.dram_tensor("SIN", [RC // 2, 128, S], F32, kind="ExternalInput").ap()
    CONSTS = nc.dram_tensor("CONSTS", [128, 4], F32, kind="ExternalInput").ap()
    ONESCOL = nc.dram_tensor("ONESCOL", [128, 1], F32, kind="ExternalInput").ap()
    ONESROW = nc.dram_tensor("ONESROW", [1, 128], F32, kind="ExternalInput").ap()
    IDENT = nc.dram_tensor("IDENT", [128, 128], F32, kind="ExternalInput").ap()
    OUT = nc.dram_tensor("OUT", [KC, 128, S], F32, kind="ExternalOutput").ap()
    ikind = "ExternalOutput" if debug else "Internal"
    TGV = nc.dram_tensor("TGV", [GC, 128, KC * 128], BF16, kind=ikind).ap()
    TO2 = nc.dram_tensor("TO2", [OC, 128, HC * 128], BF16, kind=ikind).ap()
    QBF = nc.dram_tensor("QBF", [NTB, 128, KC * TB], BF16, kind=ikind).ap()
    DBG = nc.dram_tensor("DBG", [128, 64], F32, kind="ExternalOutput").ap() if debug else None

    P = _Prog()
    es = contextlib.ExitStack()
    with es:
        arA = es.enter_context(nc.sbuf_tensor("arA", [128, 8192], F32))
        arB = es.enter_context(nc.sbuf_tensor("arB", [128, 6144], F32))
        arC = es.enter_context(nc.sbuf_tensor("arC", [128, 8192], F32))
        arD = es.enter_context(nc.sbuf_tensor("arD", [128, 6144], F32))
        arF = es.enter_context(nc.sbuf_tensor("arF", [128, 8192], F32))
        arG = es.enter_context(nc.sbuf_tensor("arG", [128, 8192], F32))
        wsl = es.enter_context(nc.sbuf_tensor("wsl", [128, 2048], F32))
        sml = es.enter_context(nc.sbuf_tensor("sml", [128, 176], F32))
        row = es.enter_context(nc.sbuf_tensor("rowbuf", [1, 3840], F32))
        idb = es.enter_context(nc.sbuf_tensor("idb", [128, 128], F32))
        ps = [es.enter_context(nc.psum_tensor(f"pb{i}", [128, 512], F32)) for i in range(8)]

        mksem = lambda name: _Sem(es.enter_context(nc.semaphore(name)))
        dPL = mksem("dPL")
        dSL = [mksem("dSL0"), mksem("dSL1")]
        dSLg = [mksem("dSLg0"), mksem("dSLg1")]
        dO2 = [mksem("dO20"), mksem("dO21")]
        dXT = mksem("dXT")
        dCOS = mksem("dCOS")
        dSIN = mksem("dSIN")
        dXB = mksem("dXB")
        dQB = mksem("dQB")
        dQS = mksem("dQS")
        dTS = [mksem("dTS0"), mksem("dTS1")]
        dOS = [mksem("dOS0"), mksem("dOS1")]
        dDB = mksem("dDB")
        gG = mksem("gG")
        tT = mksem("tT")
        vV = mksem("vV")
        sS = mksem("sS")
        block = es.enter_context(nc.Block())

        CT = sml[:, 0:4]
        ONEC = sml[:, 4:5]
        ONECB = sml[:, 5:6].bitcast(BF16)[:, 0:1]
        BINc = sml[:, 8:56]
        BOUTc = sml[:, 56:72]
        Gc = sml[:, 72:88]
        Acol = sml[:, 88:104]
        FLAG = sml[:, 104:105]
        RSGV = sml[:, 105:106]
        RSO2 = sml[:, 106:107]
        ONER = row[0:1, 0:128]
        SWGV = row[0:1, 132:133]
        SWO2 = row[0:1, 133:134]
        for v, c in ((1.0, 0), (EPS, 1), (-0.5, 2), (0.5, 3)):
            nc.const_aps.register(F32, v, CT[:, c:c + 1])

        for dst, src in ((sml[:, 0:4], CONSTS), (sml[:, 4:5], ONESCOL),
                         (row[0:1, 0:128], ONESROW), (idb[:], IDENT),
                         (BINc, BIN), (BOUTc, BOUT), (Gc, G)):
            P.run("sync", lambda e, dst=dst, src=src: e.dma_start(out=dst, in_=src),
                  inc=dPL, inc_n=16)
        P.wait("vector", dPL)
        P.run("vector", lambda e: e.tensor_scalar(FLAG, ONEC, M64, M64, ALU.add, ALU.subtract))
        P.run("vector", lambda e: e.tensor_copy(ONECB, ONEC))
        P.run("vector", lambda e: e.memset(Acol, 0.0), inc=vV)

        bank_tok = [None] * 8

        def bank_wait(bank):
            t = bank_tok[bank]
            if t is not None:
                P.wait("tensor", t[0], t[1])

        def bank_read(bank, sem):
            bank_tok[bank] = (sem, sem.n)

        arC_bfq = arC[:].bitcast(BF16)
        arG_bfq = arG[:].bitcast(BF16)
        arB_bfq = arB[:].bitcast(BF16)
        arD_bfq = arD[:].bitcast(BF16)
        qp_in = [arF[:, 0:8192], arA[:, 0:8192]]
        qp_absl = [arC_bfq[:, 0:8192], arC_bfq[:, 8192:16384]]
        qp_t1 = [arG_bfq[:, 0:8192], arC_bfq[:, 0:8192]]
        qp_t2 = [arG_bfq[:, 8192:16384], arC_bfq[:, 8192:16384]]
        qp_out = [arD_bfq[:, 0:8192], arB_bfq[:, 0:8192]]

        def _quad(img, i, nsl=4):
            w = img.shape[2]
            return bass.AP(img.tensor, img.offset + i * nsl * 128 * w,
                           [[w, 128], [128 * w, nsl], [1, w]])

        def qp_sum_pass(img, nunit, psbank):
            mm_of = {}
            abs_of = {}
            P.wait("sync", sS)
            P.wait("scalar", tT)
            for i in range(nunit):
                b = i % 2
                if i >= 2:
                    P.wait("sync", sS, abs_of[i - 2])
                P.run("sync", lambda e, i=i, b=b, img=img: e.dma_start(out=qp_in[b], in_=img(i)),
                      inc=dSL[b], inc_n=16)
                P.wait("scalar", dSL[b])
                if i >= 2:
                    P.wait("scalar", tT, mm_of[i - 2])
                P.run("scalar", lambda e, b=b: e.activation(qp_absl[b], qp_in[b], AF.Abs,
                                                            bias=0.0, scale=1.0), inc=sS)
                abs_of[i] = sS.n
                P.wait("tensor", sS)
                for j in range(16):
                    P.run("tensor", lambda e, b=b, j=j, i=i, psbank=psbank, nunit=nunit:
                          e.matmul(ps[psbank][0:1, 0:512], ONECB,
                                   qp_absl[b][:, 512 * j:512 * (j + 1)],
                                   start=(i == 0 and j == 0),
                                   stop=(i == nunit - 1 and j == 15)),
                          inc=(tT if j == 15 else None))
                mm_of[i] = tT.n

        def qp_finish(psbank, recip_col, srow_cell, count):
            P.wait("vector", tT)
            r0 = row[0:1, 136 + psbank:137 + psbank]
            P.run("vector", lambda e: e.reduce_sum(r0, ps[psbank][0:1, 0:512], axis=AX.X),
                  inc=vV)
            P.wait("vector", vV)
            P.run("vector", lambda e: e.tensor_scalar(srow_cell, r0, 1.0 / count, 1e-5,
                                                      ALU.mult, ALU.max), inc=vV)
            P.wait("vector", vV)
            rc = row[0:1, 140 + psbank:141 + psbank]
            P.run("vector", lambda e: e.reciprocal(rc, srow_cell), inc=vV)
            P.wait("tensor", vV)
            bank_wait(5)
            P.run("tensor", lambda e: e.matmul(ps[5][:, 0:1], ONER, rc, start=True, stop=True),
                  inc=tT)
            P.wait("vector", tT)
            P.run("vector", lambda e: e.tensor_copy(recip_col, ps[5][:, 0:1]), inc=vV)
            bank_read(5, vV)

        tn_in = [arF[:, 0:4096], arF[:, 4096:8192]]
        arG_bft = arG[:].bitcast(BF16)
        arD_bft = arD[:].bitcast(BF16)
        tn_t1 = [arG_bft[:, 0:4096], arG_bft[:, 8192:12288]]
        tn_t2 = [arG_bft[:, 4096:8192], arG_bft[:, 12288:16384]]
        tn_out = [arD_bft[:, 8192:12288], arD_bft[:, 8192:12288]]


        gv_in2 = lambda i: _quad(WGV, i, 2)
        gv_out2 = lambda i: _quad(TGV, i, 2)
        o2_in2 = lambda i: WO2[i // 2][:, 4096 * (i % 2):4096 * (i % 2 + 1)]
        o2_out2 = lambda i: TO2[i // 2][:, 4096 * (i % 2):4096 * (i % 2 + 1)]
        TN_UNITS = [(gv_in2, gv_out2, GC // 2, RSGV), (o2_in2, o2_out2, 2 * OC, RSO2)]
        tn_state = {"pass": 0, "i": 0, "sgn": {}, "add": {}}

        def tern_step():
            st = tn_state
            if st["pass"] >= len(TN_UNITS):
                return
            img, out_img, nunit, recip_col = TN_UNITS[st["pass"]]
            i, b = st["i"], st["i"] % 2
            key = (st["pass"], i - 2)
            use_v = False
            if key in st["sgn"]:
                P.wait("sync", sS, st["sgn"][key][0])
                P.wait("sync", vV, st["sgn"][key][1])
            P.run("sync", lambda e, i=i, b=b, img=img: e.dma_start(out=tn_in[b], in_=img(i)),
                  inc=dSL[b], inc_n=16)
            if use_v:
                P.wait("vector", dSL[b])
                if key in st["add"]:
                    P.wait("vector", vV, st["add"][key])
                P.run("vector", lambda e, b=b, recip_col=recip_col: e.tensor_scalar(
                    tn_t1[b], tn_in[b], recip_col, 0.5, ALU.mult, ALU.is_ge))
                P.run("vector", lambda e, b=b, recip_col=recip_col: e.tensor_scalar(
                    tn_t2[b], tn_in[b], recip_col, -0.5, ALU.mult, ALU.is_le), inc=vV)
                st["sgn"][(st["pass"], i)] = (sS.n, vV.n)
                P.wait("vector", dTS[0])
                P.run("vector", lambda e, b=b: e.tensor_tensor(
                    tn_t1[b], tn_t1[b], tn_t2[b], ALU.subtract))
                P.run("vector", lambda e, b=b: e.tensor_scalar(
                    tn_out[b], tn_t1[b], 2.0, None, ALU.mult), inc=vV)
            else:
                P.wait("scalar", dSL[b])
                if key in st["add"]:
                    P.wait("scalar", vV, st["add"][key])
                P.run("scalar", lambda e, b=b, recip_col=recip_col: e.activation(
                    tn_t1[b], tn_in[b], AF.Sign, bias=-0.5, scale=recip_col))
                P.run("scalar", lambda e, b=b, recip_col=recip_col: e.activation(
                    tn_t2[b], tn_in[b], AF.Sign, bias=0.5, scale=recip_col), inc=sS)
                st["sgn"][(st["pass"], i)] = (sS.n, vV.n)
                P.wait("vector", sS)
                P.wait("vector", dTS[0])
                P.run("vector", lambda e, b=b: e.tensor_tensor(tn_out[b], tn_t1[b], tn_t2[b],
                                                               ALU.add), inc=vV)
            st["add"][(st["pass"], i)] = vV.n
            P.wait("sync", vV)
            P.run("sync", lambda e, i=i, b=b, out_img=out_img:
                  e.dma_start(out=out_img(i), in_=tn_out[b]), inc=dTS[0], inc_n=16)
            st["i"] += 1
            if st["i"] >= nunit:
                st["pass"] += 1
                st["i"] = 0

        def tern_flush():
            while tn_state["pass"] < len(TN_UNITS):
                tern_step()

        P.wait("scalar", dPL)
        P.wait("tensor", vV)
        gv_in = lambda i: _quad(WGV, i)
        o2_in = lambda i: WO2[i]
        qp_sum_pass(gv_in, GC // 4, 2)
        qp_finish(2, RSGV, SWGV, GC * 128 * 2048)
        qp_sum_pass(o2_in, OC, 3)
        qp_finish(3, RSO2, SWO2, OC * 128 * 8192)
        P.wait("gpsimd", sS)
        P.wait("gpsimd", vV)

        arC_bf = arC[:].bitcast(BF16)
        arA_bf = arA[:].bitcast(BF16)
        wslb = wsl[:].bitcast(BF16)
        slab = [wslb[:, 0:2048], wslb[:, 2048:4096]]
        slab_cons = [0, 0]
        slab_sem = [dSL[0], dSL[1]]

        def load_slab(img, c, si, eng="gpsimd"):
            b = si % 2
            if slab_cons[b] > 0:
                P.wait(eng, tT, slab_cons[b])
            sem = dSLg[b] if eng == "gpsimd" else dSL[b]
            slab_sem[b] = sem
            P.run(eng, lambda e, img=img, c=c, b=b: e.dma_start(out=slab[b], in_=img[c]),
                  inc=sem, inc_n=16)
            return b

        def mm_chunk(b, rhs_bf, psbank, extra_wait=None):
            P.wait("tensor", slab_sem[b])
            bank_wait(psbank)
            if extra_wait is not None:
                P.wait("tensor", extra_wait)
            rhs_of = rhs_bf if callable(rhs_bf) else (
                lambda k, r=rhs_bf: r[:, TB * k:TB * (k + 1)])
            for k in range(KC):
                P.run("tensor", lambda e, b=b, k=k, psbank=psbank, rhs_of=rhs_of:
                      e.matmul(ps[psbank][:, 0:TB], slab[b][:, 128 * k:128 * (k + 1)],
                               rhs_of(k), start=(k == 0), stop=(k == KC - 1)),
                      inc=(tT if k == KC - 1 else None))

        XTC = arC_bf

        def xt_tile(tl, k):
            return XTC[:, 8192 * tl + 512 * k:8192 * tl + 512 * (k + 1)]

        def mm_group(b, tl, bank):
            P.wait("tensor", slab_sem[b])
            bank_wait(bank)
            for k in range(KC):
                P.run("tensor", lambda e, b=b, tl=tl, k=k, bank=bank:
                      e.matmul(ps[bank][:, 0:TB], slab[b][:, 128 * k:128 * (k + 1)],
                               xt_tile(tl, k), start=(k == 0), stop=(k == KC - 1)),
                      inc=(tT if k == KC - 1 else None))

        arD_bf = arD[:].bitcast(BF16)
        COSb = arD_bf[:, 0:1024]
        SINb = arD_bf[:, 1024:2048]
        rBt = [arD[:, 2048 + 512 * t:2048 + 512 * (t + 1)] for t in range(NTB)]
        si = [0]
        HW_ = 1024

        KB0 = arB[:, 0:HW_]
        KB1 = arB[:, HW_:2 * HW_]
        VST = arA[:, 0:HW_]
        TM1 = arA[:, 2048:2048 + HW_]
        TM2 = arA[:, 4096:4096 + HW_]
        PBUF = arA[:, 6144:6144 + HW_]
        SRED2 = sml[:, 108:110]
        SRED1 = sml[:, 112:113]
        QEb = arA_bf[:, 0:HW_]
        QOb = arA_bf[:, HW_:2 * HW_]
        QPb = arA_bf[:, 2 * HW_:3 * HW_]
        sqb = [arA[:, 512 * t:512 * (t + 1)] for t in range(2)]

        for hf in range(2):
            gtb = (2 * hf, 2 * hf + 1)
            if hf > 0:
                P.wait("gpsimd", tT)
            for tl in range(2):
                P.run("gpsimd", lambda e, hf=hf, tl=tl: e.dma_start(
                    out=XTC[:, 8192 * tl:8192 * (tl + 1)], in_=XT[2 * hf + tl]),
                    inc=dXT, inc_n=16)

            statmm = {}
            P.wait("tensor", dXT)
            for c in range(KC):
                b = load_slab(WQKV, KC + c, si[0]); si[0] += 1
                for tl in range(2):
                    mm_group(b, tl, tl)
                    if tl == 1:
                        slab_cons[b] = tT.n
                    P.wait("scalar", tT)
                    if (c - 1, tl) in statmm:
                        P.wait("scalar", tT, statmm[(c - 1, tl)])
                    P.run("scalar", lambda e, tl=tl, c=c: e.activation(
                        sqb[tl], ps[tl][:, 0:TB], AF.Square,
                        bias=BINc[:, KC + c:KC + c + 1], scale=1.0), inc=sS)
                    bank_read(tl, sS)
                    P.wait("tensor", sS)
                    if c == 0:
                        bank_wait(4 + tl)
                    P.run("tensor", lambda e, c=c, tl=tl: e.matmul(
                        ps[4 + tl][0:1, 0:TB], ONEC, sqb[tl],
                        start=(c == 0), stop=(c == KC - 1)), inc=tT)
                    statmm[(c, tl)] = tT.n
                tern_step()
                tern_step()
            P.wait("scalar", tT)
            for tl in range(2):
                RT0 = row[0:1, 256 + 512 * tl:256 + 512 * (tl + 1)]
                P.run("scalar", lambda e, RT0=RT0, tl=tl: e.activation(
                    RT0, ps[4 + tl][0:1, 0:TB], AF.Sqrt, bias=0.0, scale=1.0), inc=sS)
                bank_read(4 + tl, sS)
            P.wait("vector", sS)
            for tl in range(2):
                RT0 = row[0:1, 256 + 512 * tl:256 + 512 * (tl + 1)]
                P.run("vector", lambda e, RT0=RT0: e.tensor_scalar(RT0, RT0, 1e-12, None,
                                                                   ALU.max))
                P.run("vector", lambda e, RT0=RT0: e.reciprocal(RT0, RT0),
                      inc=(vV if tl == 1 else None))
            P.wait("tensor", vV)
            for tl in range(2):
                RT0 = row[0:1, 256 + 512 * tl:256 + 512 * (tl + 1)]
                bank_wait(tl)
                P.run("tensor", lambda e, RT0=RT0, tl=tl: e.matmul(
                    ps[tl][:, 0:TB], ONER, RT0, start=True, stop=True), inc=tT)
            P.wait("scalar", tT)
            for tl in range(2):
                P.run("scalar", lambda e, tl=tl, hf=hf: e.copy(rBt[2 * hf + tl],
                                                               ps[tl][:, 0:TB]), inc=sS)
                bank_read(tl, sS)

            kb_read = [0, 0]
            rope_done = 0

            def proj_chunk(wcol, dstbuf, bias_col, read_guard=None):
                b = load_slab(WQKV, wcol, si[0]); si[0] += 1
                for tl in range(2):
                    mm_group(b, tl, tl)
                    if tl == 1:
                        slab_cons[b] = tT.n
                    P.wait("vector", tT)
                    if tl == 0 and read_guard:
                        P.wait("vector", vV, read_guard)
                    P.run("vector", lambda e, tl=tl, dstbuf=dstbuf, bias_col=bias_col:
                          e.tensor_scalar(dstbuf[:, 512 * tl:512 * (tl + 1)],
                                          ps[tl][:, 0:TB], bias_col, None, ALU.add),
                          inc=vV)
                    bank_read(tl, vV)
                tern_step()

            def rope_pair(E, O, outE=None, outO=None):
                P.run("vector", lambda e, O=O: e.tensor_tensor(TM1, O, SINb, ALU.mult))
                P.run("vector", lambda e, E=E: e.tensor_tensor(TM2, E, SINb, ALU.mult))
                P.run("vector", lambda e, E=E: e.tensor_tensor(E, E, COSb, ALU.mult))
                P.run("vector", lambda e, E=E, outE=outE: e.tensor_tensor(
                    outE if outE is not None else E, E, TM1, ALU.subtract))
                P.run("vector", lambda e, O=O: e.tensor_tensor(O, O, COSb, ALU.mult))
                P.run("vector", lambda e, O=O, outO=outO: e.tensor_tensor(
                    outO if outO is not None else O, O, TM2, ALU.add), inc=vV)

            def a_accum(krbuf, cv):
                P.run("vector", lambda e, krbuf=krbuf: e.tensor_tensor(PBUF, krbuf, VST,
                                                                       ALU.mult))
                for tl in range(2):
                    P.run("vector", lambda e, tl=tl, hf=hf: e.tensor_tensor(
                        PBUF[:, 512 * tl:512 * (tl + 1)], PBUF[:, 512 * tl:512 * (tl + 1)],
                        rBt[2 * hf + tl], ALU.mult))
                p3 = bass.AP(arA[:].tensor, arA[:].offset + 6144,
                             [[8192, 128], [512, 2], [1, 512]])
                P.run("vector", lambda e, p3=p3: e.reduce_sum(SRED2, p3, axis=AX.X), inc=vV)
                P.wait("vector", vV)
                P.run("vector", lambda e: e.reduce_sum(SRED1, SRED2, axis=AX.X), inc=vV)
                P.wait("vector", vV)
                P.run("vector", lambda e, cv=cv: e.tensor_tensor(
                    Acol[:, cv:cv + 1], Acol[:, cv:cv + 1], SRED1, ALU.add), inc=vV)

            for pb in range(RC // 2):
                c0, c1 = 2 * pb, 2 * pb + 1
                P.wait("gpsimd", vV, rope_done)
                P.run("gpsimd", lambda e, pb=pb, hf=hf: e.dma_start(
                    out=COSb, in_=COS[pb][:, HW_ * hf:HW_ * (hf + 1)]), inc=dCOS, inc_n=16)
                P.run("gpsimd", lambda e, pb=pb, hf=hf: e.dma_start(
                    out=SINb, in_=SIN[pb][:, HW_ * hf:HW_ * (hf + 1)]), inc=dSIN, inc_n=16)
                proj_chunk(KC + c0, KB0, BINc[:, KC + c0:KC + c0 + 1], read_guard=kb_read[0])
                proj_chunk(KC + c1, KB1, BINc[:, KC + c1:KC + c1 + 1], read_guard=kb_read[1])
                P.wait("vector", dCOS)
                P.wait("vector", dSIN)
                rope_pair(KB0, KB1)
                rope_done = vV.n
                proj_chunk(2 * KC + c0, VST, BINc[:, 2 * KC + c0:2 * KC + c0 + 1],
                           read_guard=vV.n)
                a_accum(KB0, c0)
                proj_chunk(2 * KC + c1, VST, BINc[:, 2 * KC + c1:2 * KC + c1 + 1],
                           read_guard=vV.n)
                a_accum(KB1, c1)
                kb_read = [vV.n, vV.n]
            for c in range(RC, KC):
                proj_chunk(KC + c, KB0, BINc[:, KC + c:KC + c + 1], read_guard=kb_read[0])
                proj_chunk(2 * KC + c, VST, BINc[:, 2 * KC + c:2 * KC + c + 1],
                           read_guard=vV.n)
                a_accum(KB0, c)
                kb_read[0] = vV.n

            def q_spill_chunk(src_bf, c):
                for tl in range(2):
                    P.run("sync", lambda e, tl=tl, c=c, hf=hf, src_bf=src_bf: e.dma_start(
                        out=QBF[2 * hf + tl][:, 512 * c:512 * (c + 1)],
                        in_=src_bf[:, 512 * tl:512 * (tl + 1)]), inc=dQS, inc_n=16)

            q_guard = kb_read[0]
            for pb in range(RC // 2):
                c0, c1 = 2 * pb, 2 * pb + 1
                P.wait("gpsimd", vV, rope_done)
                P.run("gpsimd", lambda e, pb=pb, hf=hf: e.dma_start(
                    out=COSb, in_=COS[pb][:, HW_ * hf:HW_ * (hf + 1)]), inc=dCOS, inc_n=16)
                P.run("gpsimd", lambda e, pb=pb, hf=hf: e.dma_start(
                    out=SINb, in_=SIN[pb][:, HW_ * hf:HW_ * (hf + 1)]), inc=dSIN, inc_n=16)
                proj_chunk(c0, KB0, BINc[:, c0:c0 + 1], read_guard=q_guard)
                proj_chunk(c1, KB1, BINc[:, c1:c1 + 1])
                P.wait("vector", dCOS)
                P.wait("vector", dSIN)
                P.wait("vector", dQS)
                rope_pair(KB0, KB1, outE=QEb, outO=QOb)
                rope_done = vV.n
                q_guard = vV.n
                P.wait("sync", vV)
                q_spill_chunk(QEb, c0)
                q_spill_chunk(QOb, c1)
            for c in range(RC, KC):
                b = load_slab(WQKV, c, si[0]); si[0] += 1
                for tl in range(2):
                    mm_group(b, tl, tl)
                    if tl == 1:
                        slab_cons[b] = tT.n
                    P.wait("vector", tT)
                    if tl == 0:
                        P.wait("vector", dQS)
                    P.run("vector", lambda e, tl=tl, c=c: e.tensor_scalar(
                        QPb[:, 512 * tl:512 * (tl + 1)], ps[tl][:, 0:TB],
                        BINc[:, c:c + 1], None, ALU.add), inc=vV)
                    bank_read(tl, vV)
                tern_step()
                P.wait("sync", vV)
                q_spill_chunk(QPb, c)

        tern_flush()
        ph1_t, ph1_v, ph1_s = tT.n, vV.n, sS.n

        arG_bf = arG[:].bitcast(BF16)
        Qbf = arG_bf[:, 0:KC * TB]
        yslot = [arG_bf[:, KC * TB:KC * TB + TB]] + \
                [arG_bf[:, TB * c:TB * (c + 1)] for c in range(KC - 1)]
        Xbf2 = arC_bf[:, 0:KC * TB]
        YB = arA
        X1 = arC
        arD_bf2 = arD[:].bitcast(BF16)
        Q1 = arD_bf2[:, 4096:4096 + KC * TB]
        arF_bf2 = arF[:].bitcast(BF16)
        wslot = [arA[:].bitcast(BF16)[:, TB * i:TB * (i + 1)] for i in range(32)] + \
                [arF_bf2[:, TB * i:TB * (i + 1)] for i in range(32)]
        o2sl = [arG_bf[:, 0:HC * 128], arG_bf[:, HC * 128:2 * HC * 128]]
        TMP = [arB[:, 0:512], arB[:, 512:1024]]
        WMXr = arB[:, 3072:3072 + TB]
        WMNr = arB[:, 3584:3584 + TB]

        RM = row[0:1, 256:256 + TB]
        RMSQ = row[0:1, 768:768 + TB]
        RVAR = row[0:1, 1280:1280 + TB]
        RSTD = row[0:1, 1792:1792 + TB]
        RB1 = row[0:1, 2304:2304 + TB]
        RSTDr = row[0:1, 2816:2816 + TB]
        MROW = row[0:1, 3328:3328 + TB]
        F1B = arD[:, 0:TB]
        B1B = arD[:, TB:2 * TB]
        PSIB = arD[:, 2 * TB:3 * TB]
        ALPB = arD[:, 3 * TB:4 * TB]

        def bcast_row(src_row, dst, bank):
            P.wait("tensor", vV)
            P.wait("tensor", sS)
            bank_wait(bank)
            P.run("tensor", lambda e, src_row=src_row, bank=bank: e.matmul(
                ps[bank][:, 0:TB], ONER, src_row, start=True, stop=True), inc=tT)
            P.wait("scalar", tT)
            P.run("scalar", lambda e, dst=dst, bank=bank: e.copy(dst, ps[bank][:, 0:TB]),
                  inc=sS)
            bank_read(bank, sS)

        def colmax_to_row(src_tile, dst_row):
            P.wait("tensor", vV)
            bank_wait(5)
            for j in range(TB // 128):
                P.run("tensor", lambda e, j=j: e.transpose(
                    ps[5][:, 128 * j:128 * (j + 1)], src_tile[:, 128 * j:128 * (j + 1)],
                    idb[:]), inc=(tT if j == TB // 128 - 1 else None))
            P.wait("vector", tT)
            for j in range(TB // 128):
                P.run("vector", lambda e, j=j: e.reduce_max(
                    arB[:, 4096 + 2 * j:4096 + 2 * j + 1], ps[5][:, 128 * j:128 * (j + 1)],
                    axis=AX.X), inc=(vV if j == TB // 128 - 1 else None))
            bank_read(5, vV)
            P.wait("tensor", vV)
            bank_wait(6)
            for j in range(TB // 128):
                P.run("tensor", lambda e, j=j: e.transpose(
                    ps[6][0:1, 128 * j:128 * (j + 1)], arB[:, 4096 + 2 * j:4096 + 2 * j + 1],
                    idb[:]), inc=(tT if j == TB // 128 - 1 else None))
            P.wait("scalar", tT)
            P.run("scalar", lambda e, dst_row=dst_row: e.copy(dst_row, ps[6][0:1, 0:TB]),
                  inc=sS)
            bank_read(6, sS)

        for tb in range(NTB):
            if tb == 0:
                P.wait("sync", tT, ph1_t)
                P.wait("sync", vV, ph1_v)
                P.wait("sync", dQS)
                P.wait("sync", dTS[0])
                P.wait("sync", dTS[1])
                P.wait("gpsimd", tT, ph1_t)
                P.wait("gpsimd", vV, ph1_v)
            else:
                P.wait("sync", tT)
                P.wait("sync", vV)
                P.wait("gpsimd", vV)
            P.run("sync", lambda e, tb=tb: e.dma_start(out=Qbf, in_=QBF[tb]), inc=dQB, inc_n=16)
            P.run("gpsimd", lambda e, tb=tb: e.dma_start(out=Xbf2, in_=XT[tb]), inc=dXB, inc_n=16)

            P.wait("vector", dQB)
            for c in range(KC):
                P.run("vector", lambda e, c=c: e.tensor_scalar(
                    yslot[c], Qbf[:, TB * c:TB * (c + 1)],
                    Acol[:, c:c + 1], None, ALU.mult), inc=(vV if c == KC - 1 else None))

            for c in range(OC):
                b = load_slab(WOUT, c, si[0]); si[0] += 1
                mm_chunk(b, (lambda k: yslot[k]), c % 4, extra_wait=(vV if c == 0 else None))
                slab_cons[b] = tT.n
                P.wait("scalar", tT)
                if c == 0:
                    P.wait("scalar", vV)
                    P.wait("scalar", tT)
                P.run("scalar", lambda e, c=c: e.activation(
                    YB[:, TB * c:TB * (c + 1)], ps[c % 4][:, 0:TB], AF.Identity,
                    bias=BOUTc[:, c:c + 1], scale=1.0), inc=sS)
                bank_read(c % 4, sS)
            P.wait("vector", sS)
            P.wait("vector", dXB)
            P.run("vector", lambda e: e.scalar_tensor_tensor(
                YB[:, 0:KC * TB], Xbf2, 2.0, YB[:, 0:KC * TB], ALU.mult, ALU.add), inc=vV)
            sq = [arB[:, 1024:1024 + TB], arB[:, 1536:1536 + TB]]
            sqmm = {}
            P.wait("scalar", vV)
            for c in range(KC):
                bq = c % 2
                if c >= 2:
                    P.wait("scalar", tT, sqmm[c - 2])
                P.run("scalar", lambda e, c=c, bq=bq: e.activation(
                    sq[bq], YB[:, TB * c:TB * (c + 1)], AF.Square, bias=0.0, scale=1.0),
                    inc=sS)
                P.wait("tensor", sS)
                if c == 0:
                    bank_wait(2)
                    bank_wait(3)
                P.run("tensor", lambda e, c=c: e.matmul(
                    ps[2][0:1, 0:TB], ONEC, YB[:, TB * c:TB * (c + 1)],
                    start=(c == 0), stop=(c == KC - 1)))
                P.run("tensor", lambda e, c=c, bq=bq: e.matmul(
                    ps[3][0:1, 0:TB], ONEC, sq[bq],
                    start=(c == 0), stop=(c == KC - 1)), inc=tT)
                sqmm[c] = tT.n
            P.wait("scalar", tT)
            P.run("scalar", lambda e: e.activation(RM, ps[2][0:1, 0:TB], AF.Identity,
                                                   bias=0.0, scale=1.0 / D))
            P.run("scalar", lambda e: e.activation(RMSQ, ps[3][0:1, 0:TB], AF.Identity,
                                                   bias=0.0, scale=1.0 / D), inc=sS)
            bank_read(2, sS)
            bank_read(3, sS)
            P.wait("vector", sS)
            P.run("vector", lambda e: e.tensor_tensor(RVAR, RM, RM, ALU.mult))
            P.run("vector", lambda e: e.tensor_tensor(RVAR, RMSQ, RVAR, ALU.subtract), inc=vV)
            P.wait("scalar", vV)
            P.run("scalar", lambda e: e.activation(RSTD, RVAR, AF.Sqrt, bias=EPS, scale=1.0),
                  inc=sS)
            P.wait("vector", sS)
            P.run("vector", lambda e: e.reciprocal(RSTDr, RSTD))
            P.run("vector", lambda e: e.tensor_tensor(RB1, RM, RSTDr, ALU.mult), inc=vV)
            bcast_row(RSTDr, F1B, 4)
            bcast_row(RB1, B1B, 5)
            P.wait("vector", sS)
            f1rep = _rep0(arD[:], 0, KC, TB)
            b1rep = _rep0(arD[:], TB, KC, TB)
            P.run("vector", lambda e, f1rep=f1rep: e.tensor_tensor(
                X1[:, 0:KC * TB], YB[:, 0:KC * TB], f1rep, ALU.mult))
            P.run("vector", lambda e, b1rep=b1rep: e.tensor_tensor(
                X1[:, 0:KC * TB], X1[:, 0:KC * TB], b1rep, ALU.subtract), inc=vV)

            for c in range(KC):
                P.run("vector", lambda e, c=c: e.tensor_scalar(
                    YB[:, TB * c:TB * (c + 1)], X1[:, TB * c:TB * (c + 1)],
                    Gc[:, c:c + 1], None, ALU.mult), inc=(vV if c == KC - 1 else None))
            MX = arB[:, 2048:2048 + TB]
            MN = arB[:, 2560:2560 + TB]
            P.run("vector", lambda e: e.tensor_tensor(MX, YB[:, 0:TB], YB[:, TB:2 * TB], ALU.max))
            P.run("vector", lambda e: e.tensor_tensor(MN, YB[:, 0:TB], YB[:, TB:2 * TB], ALU.min))
            for c in range(2, KC):
                P.run("vector", lambda e, c=c: e.tensor_tensor(
                    MX, MX, YB[:, TB * c:TB * (c + 1)], ALU.max))
                P.run("vector", lambda e, c=c: e.tensor_tensor(
                    MN, MN, YB[:, TB * c:TB * (c + 1)], ALU.min))
            P.run("vector", lambda e: e.tensor_scalar(MN, MN, -1.0, None, ALU.mult))
            P.run("vector", lambda e: e.tensor_tensor(MX, MX, MN, ALU.max), inc=vV)
            colmax_to_row(MX, MROW)
            RD = RSTD
            RDr = RB1
            P.wait("vector", sS)
            P.run("vector", lambda e: e.tensor_scalar(RD, RVAR, EPS, None, ALU.add))
            P.run("vector", lambda e: e.reciprocal(RDr, RD))
            P.run("vector", lambda e: e.tensor_tensor(RD, RVAR, RDr, ALU.mult), inc=vV)
            P.wait("scalar", vV)
            P.run("scalar", lambda e: e.activation(RDr, RD, AF.Sqrt, bias=EPS, scale=1.0),
                  inc=sS)
            P.wait("vector", sS)
            P.run("vector", lambda e: e.reciprocal(RD, RDr))
            RMAXH = RMSQ
            P.run("vector", lambda e: e.tensor_tensor(RMAXH, RD, MROW, ALU.mult))
            P.run("vector", lambda e: e.tensor_scalar(RMAXH, RMAXH, 1e-5, None, ALU.max))
            RS1 = RM
            P.run("vector", lambda e: e.reciprocal(RS1, RMAXH))
            P.run("vector", lambda e: e.tensor_scalar(RS1, RS1, QB, None, ALU.mult))
            RPSI = RVAR
            P.run("vector", lambda e: e.tensor_tensor(RPSI, RD, RS1, ALU.mult))
            RALPHA = RSTDr
            P.run("vector", lambda e: e.tensor_scalar(
                RALPHA, RMAXH, SWGV, 0.5 / QB, ALU.mult, ALU.mult), inc=vV)
            bcast_row(RPSI, PSIB, 4)
            bcast_row(RALPHA, ALPB, 5)
            P.wait("vector", sS)
            psirep = _rep0(arD[:], 2 * TB, KC, TB)
            P.run("vector", lambda e, psirep=psirep: e.tensor_tensor(
                YB[:, 0:KC * TB], YB[:, 0:KC * TB], psirep, ALU.mult), inc=vV)
            for c in range(KC):
                zc = YB[:, TB * c:TB * (c + 1)]
                P.run("vector", lambda e, zc=zc: e.tensor_scalar(
                    TMP[0][:, 0:TB], zc, M32, M32, ALU.add, ALU.subtract))
                P.run("vector", lambda e, zc=zc: e.tensor_scalar(
                    TMP[1][:, 0:TB], zc, M64, M64, ALU.add, ALU.subtract))
                P.run("vector", lambda e: e.tensor_tensor(
                    TMP[1][:, 0:TB], TMP[1][:, 0:TB], TMP[0][:, 0:TB], ALU.subtract))
                P.run("vector", lambda e, c=c: e.scalar_tensor_tensor(
                    Q1[:, TB * c:TB * (c + 1)], TMP[1][:, 0:TB], FLAG[:, 0:1],
                    TMP[0][:, 0:TB], ALU.mult, ALU.add), inc=(vV if c == KC - 1 else None))

            WMX = WMXr
            WMN = WMNr
            P.run("vector", lambda e: e.memset(WMX, 0.0))
            P.run("vector", lambda e: e.memset(WMN, 0.0), inc=vV)
            gvg = [arB[:, 4608:4608 + TB], arB[:, 5120:5120 + TB]]
            gvs = [TMP[0][:, 0:TB], TMP[1][:, 0:TB]]
            for h in range(HC):
                par = h % 2
                vbank, gbank = (6, 7) if par == 0 else (4, 5)
                bv_ = load_slab(TGV, 2 * h, si[0], eng="sync"); si[0] += 1
                mm_chunk(bv_, Q1, vbank, extra_wait=(vV if h == 0 else None))
                slab_cons[bv_] = tT.n
                bg_ = load_slab(TGV, 2 * h + 1, si[0], eng="sync"); si[0] += 1
                mm_chunk(bg_, Q1, gbank)
                slab_cons[bg_] = tT.n
                P.wait("vector", tT)
                P.run("vector", lambda e, par=par, gbank=gbank: e.tensor_tensor(
                    gvg[par], ps[gbank][:, 0:TB], ALPB[:, 0:TB], ALU.mult), inc=vV)
                P.wait("scalar", vV)
                P.run("scalar", lambda e, par=par: e.activation(gvs[par], gvg[par],
                                                                AF.Sigmoid, bias=0.0, scale=1.0),
                      inc=sS)
                P.wait("vector", sS)
                P.run("vector", lambda e, par=par, vbank=vbank: e.tensor_tensor(
                    gvs[par], ps[vbank][:, 0:TB], gvs[par], ALU.mult))
                P.run("vector", lambda e, h=h, par=par: e.tensor_tensor(
                    wslot[h], gvs[par], gvg[par], ALU.mult))
                P.run("vector", lambda e, h=h: e.tensor_tensor(
                    WMX, WMX, wslot[h], ALU.max))
                P.run("vector", lambda e, h=h: e.tensor_tensor(
                    WMN, WMN, wslot[h], ALU.min), inc=vV)
                bank_read(vbank, vV)
                bank_read(gbank, vV)
            P.run("vector", lambda e: e.tensor_scalar(WMN, WMN, -1.0, None, ALU.mult))
            P.run("vector", lambda e: e.tensor_tensor(WMX, WMX, WMN, ALU.max), inc=vV)
            WROW = MROW
            colmax_to_row(WMX, WROW)
            RMAXU = RM
            P.wait("vector", sS)
            P.run("vector", lambda e: e.tensor_tensor(RMAXU, RALPHA, WROW, ALU.mult))
            P.run("vector", lambda e: e.tensor_scalar(RMAXU, RMAXU, 1e-5, None, ALU.max))
            RS2 = RMSQ
            P.run("vector", lambda e: e.reciprocal(RS2, RMAXU))
            P.run("vector", lambda e: e.tensor_scalar(RS2, RS2, QB, None, ALU.mult))
            RBETA = RVAR
            P.run("vector", lambda e: e.tensor_tensor(RBETA, RALPHA, RS2, ALU.mult))
            RGAM = RSTD
            P.run("vector", lambda e: e.tensor_scalar(
                RGAM, RMAXU, SWO2, 0.5 / QB, ALU.mult, ALU.mult), inc=vV)
            BETB = PSIB
            GAMB = ALPB
            bcast_row(RBETA, BETB, 4)
            bcast_row(RGAM, GAMB, 5)
            P.wait("vector", sS)
            for h in range(HC):
                P.run("vector", lambda e, h=h: e.tensor_tensor(
                    TMP[0][:, 0:TB], wslot[h], BETB[:, 0:TB], ALU.mult))
                P.run("vector", lambda e: e.tensor_scalar(
                    TMP[1][:, 0:TB], TMP[0][:, 0:TB], M32, M32, ALU.add, ALU.subtract))
                P.run("vector", lambda e: e.tensor_scalar(
                    TMP[0][:, 0:TB], TMP[0][:, 0:TB], M64, M64, ALU.add, ALU.subtract))
                P.run("vector", lambda e: e.tensor_tensor(
                    TMP[0][:, 0:TB], TMP[0][:, 0:TB], TMP[1][:, 0:TB], ALU.subtract))
                P.run("vector", lambda e, h=h: e.scalar_tensor_tensor(
                    wslot[h], TMP[0][:, 0:TB], FLAG[:, 0:1], TMP[1][:, 0:TB],
                    ALU.mult, ALU.add), inc=vV)

            y2sq = [arB[:, 1024:1024 + TB], arB[:, 1536:1536 + TB]]
            y2st = arB[:, 2048:2048 + TB]
            o2_cons = [0, 0]
            y2mm = {}
            o2banks = (0, 1, 4, 5)
            for c in range(OC):
                b = c % 2
                bank = o2banks[c % 4]
                P.wait("sync", tT, o2_cons[b] if o2_cons[b] else tT.n)
                P.run("sync", lambda e, c=c, b=b: e.dma_start(out=o2sl[b], in_=TO2[c]),
                      inc=dO2[b], inc_n=16)
                P.wait("tensor", dO2[b])
                if c == 0:
                    P.wait("tensor", vV)
                bank_wait(bank)
                for k in range(HC):
                    P.run("tensor", lambda e, b=b, k=k, bank=bank: e.matmul(
                        ps[bank][:, 0:TB], o2sl[b][:, 128 * k:128 * (k + 1)],
                        wslot[k], start=(k == 0), stop=(k == HC - 1)),
                        inc=(tT if k == HC - 1 else None))
                o2_cons[b] = tT.n
                P.wait("vector", tT)
                P.run("vector", lambda e, bank=bank: e.tensor_tensor(
                    y2st, ps[bank][:, 0:TB], GAMB[:, 0:TB], ALU.mult))
                P.run("vector", lambda e, c=c: e.scalar_tensor_tensor(
                    X1[:, TB * c:TB * (c + 1)], X1[:, TB * c:TB * (c + 1)], 2.0, y2st,
                    ALU.mult, ALU.add), inc=vV)
                bank_read(bank, vV)
            y2v = vV.n
            for c in range(OC):
                P.wait("scalar", vV, y2v)
                if c >= 2:
                    P.wait("scalar", tT, y2mm[c - 2])
                P.run("scalar", lambda e, c=c: e.activation(
                    y2sq[c % 2], X1[:, TB * c:TB * (c + 1)], AF.Square, bias=0.0, scale=1.0),
                    inc=sS)
                P.wait("tensor", sS)
                if c == 0:
                    bank_wait(2)
                    bank_wait(3)
                P.run("tensor", lambda e, c=c: e.matmul(
                    ps[2][0:1, 0:TB], ONEC, X1[:, TB * c:TB * (c + 1)],
                    start=(c == 0), stop=(c == OC - 1)))
                P.run("tensor", lambda e, c=c: e.matmul(
                    ps[3][0:1, 0:TB], ONEC, y2sq[c % 2],
                    start=(c == 0), stop=(c == OC - 1)), inc=tT)
                y2mm[c] = tT.n
            P.wait("scalar", tT)
            P.run("scalar", lambda e: e.activation(RM, ps[2][0:1, 0:TB], AF.Identity,
                                                   bias=0.0, scale=1.0 / D))
            P.run("scalar", lambda e: e.activation(RMSQ, ps[3][0:1, 0:TB], AF.Identity,
                                                   bias=0.0, scale=1.0 / D), inc=sS)
            bank_read(2, sS)
            bank_read(3, sS)
            P.wait("vector", sS)
            P.run("vector", lambda e: e.tensor_tensor(RVAR, RM, RM, ALU.mult))
            P.run("vector", lambda e: e.tensor_tensor(RVAR, RMSQ, RVAR, ALU.subtract), inc=vV)
            P.wait("scalar", vV)
            P.run("scalar", lambda e: e.activation(RSTD, RVAR, AF.Sqrt, bias=EPS, scale=1.0),
                  inc=sS)
            P.wait("vector", sS)
            P.run("vector", lambda e: e.reciprocal(RSTDr, RSTD))
            P.run("vector", lambda e: e.tensor_tensor(RB1, RM, RSTDr, ALU.mult), inc=vV)
            bcast_row(RSTDr, F1B, 4)
            bcast_row(RB1, B1B, 5)
            ost = [arB[:, 0:TB], arB[:, 512:512 + TB]]
            ost_st = {}
            P.wait("vector", sS)
            for c in range(KC):
                b = c % 2
                if c >= 2:
                    P.wait("vector", dOS[b])
                P.run("vector", lambda e, c=c, b=b: e.tensor_tensor(
                    ost[b], X1[:, TB * c:TB * (c + 1)], F1B[:, 0:TB], ALU.mult))
                P.run("vector", lambda e, c=c, b=b: e.tensor_tensor(
                    ost[b], ost[b], B1B[:, 0:TB], ALU.subtract), inc=vV)
                P.wait("sync", vV)
                P.run("sync", lambda e, c=c, b=b, tb=tb: e.dma_start(
                    out=OUT[c][:, TB * tb:TB * (tb + 1)], in_=ost[b]), inc=dOS[b], inc_n=16)
                ost_st[c] = dOS[b].n

        if debug:
            P.run("sync", lambda e: e.dma_start(out=DBG[:, 0:16], in_=Acol), inc=dDB, inc_n=16)
            P.run("sync", lambda e: e.dma_start(out=DBG[:, 16:19], in_=sml[:, 104:107],
                                                allow_slow_non_contiguous=True),
                  inc=dDB, inc_n=16)
        for sem in (dQS, dTS[0], dTS[1], dOS[0], dOS[1], dDB):
            P.wait("sync", sem)

        for eng in _Prog.ENGS:
            getattr(block, eng)(lambda e, q=tuple(P.q[eng]): [f(e) for f in q])
    return nc


_NC_CACHE = {}


def kernel(**inputs):
    per_core = build_host(inputs)
    if "nc" not in _NC_CACHE:
        _NC_CACHE["nc"] = build_nc()
    nc = _NC_CACHE["nc"]
    res = run_bass_kernel_spmd(nc, per_core, core_ids=list(range(NCORES)))
    out = np.empty((NCORES, S, D), np.float32)
    for b in range(NCORES):
        o = np.asarray(res.results[b]["OUT"])
        out[b] = o.reshape(D, S).T
    return out


# version 11
# speedup vs baseline: 1.0008x; 1.0008x over previous
import contextlib
import numpy as np
import ml_dtypes
import concourse.bass as bass
import concourse.mybir as mybir
from concourse.bass_utils import run_bass_kernel_spmd

F32 = mybir.dt.float32
BF16 = mybir.dt.bfloat16
AF = mybir.ActivationFunctionType
ALU = mybir.AluOpType
AX = mybir.AxisListType

D = 2048
S = 2048
H = 8192
KC = 16
TB = 512
NTB = S // TB
ROT = D // 2
RC = 8
HC = 64
GC = 2 * HC
OC = 16
EPS = 1e-5
M32 = 12582912.0
M64 = 6755399441055744.0
QB = 127.0
ROPE_THETA = 10000.0
NCORES = 8


def _pi_perm():
    out = np.arange(D)
    for pos in range(ROT):
        c = pos // 128
        r = pos % 128
        out[pos] = 2 * (128 * (c // 2) + r) + (c % 2)
    return out


def _pair_tables():
    rot = np.float32(ROT)
    ar = np.arange(0, ROT, 2, dtype=np.float32)
    inv_freq = (np.float32(1.0) / np.float32(ROPE_THETA) ** (ar / rot)).astype(np.float32)
    tpos = np.arange(S, dtype=np.float32)
    freqs = (tpos[None, :] * inv_freq[:, None]).astype(np.float32)
    cosv = np.cos(freqs).astype(np.float32)
    sinv = np.sin(freqs).astype(np.float32)
    nb = RC // 2
    return (np.ascontiguousarray(cosv.reshape(nb, 128, S)),
            np.ascontiguousarray(sinv.reshape(nb, 128, S)))


def _img_tiles(mat_t):
    Dk, O = mat_t.shape
    nk, no = Dk // 128, O // 128
    a = mat_t.reshape(nk, 128, no, 128)
    return np.ascontiguousarray(a.transpose(2, 1, 0, 3).reshape(no, 128, nk * 128))


def build_host(inputs):
    X = inputs["L0_attn_X"]
    W_in = np.asarray(inputs["L0_attn_W_in"], np.float32)
    b_in = np.asarray(inputs["L0_attn_b_in"], np.float32)
    W_out = np.asarray(inputs["L0_attn_W_out"], np.float32)
    b_out = np.asarray(inputs["L0_attn_b_out"], np.float32)
    g = np.asarray(inputs["L0_ffn_norm_w"], np.float32)
    w_gv = np.asarray(inputs["L0_ffn_w_gv"], np.float32)
    w_o2 = np.asarray(inputs["L0_ffn_w_out"], np.float32)
    pi = _pi_perm()

    shared = {}
    Wqkv_t = np.concatenate([W_in[0:D][pi], W_in[D:2 * D][pi], W_in[2 * D:][pi]], axis=0).T
    shared["WQKV"] = _img_tiles(np.ascontiguousarray(Wqkv_t, dtype=np.float32))
    bqkv = np.concatenate([b_in[0:D][pi], b_in[D:2 * D][pi], b_in[2 * D:][pi]])
    shared["BIN"] = np.ascontiguousarray(bqkv.reshape(3 * KC, 128).T)
    shared["WOUT"] = _img_tiles(np.ascontiguousarray(W_out[:, pi].T, dtype=np.float32))
    shared["BOUT"] = np.ascontiguousarray(b_out.reshape(OC, 128).T)
    shared["G"] = np.ascontiguousarray(g.reshape(KC, 128).T)
    rho = np.empty(2 * H, dtype=np.int64)
    for c in range(HC):
        rho[256 * c:256 * c + 128] = np.arange(128 * c, 128 * (c + 1))
        rho[256 * c + 128:256 * c + 256] = H + np.arange(128 * c, 128 * (c + 1))
    shared["WGV"] = _img_tiles(np.ascontiguousarray(w_gv[rho].T, dtype=np.float32))
    shared["WO2"] = _img_tiles(np.ascontiguousarray(w_o2.T, dtype=np.float32))
    cos2, sin2 = _pair_tables()
    shared["COS"] = cos2
    shared["SIN"] = sin2
    shared["CONSTS"] = np.ascontiguousarray(
        np.broadcast_to(np.asarray([1.0, EPS, -0.5, 0.5], np.float32), (128, 4)))
    shared["ONESCOL"] = np.ones((128, 1), np.float32)
    shared["ONESROW"] = np.ones((1, 128), np.float32)
    shared["IDENT"] = np.eye(128, dtype=np.float32)

    per_core = []
    for b in range(NCORES):
        xb = np.asarray(X[b], np.float32).reshape(NTB, TB, KC, 128)
        img = np.ascontiguousarray(xb.transpose(0, 3, 2, 1).reshape(NTB, 128, KC * TB))
        m = {"XT": img}
        m.update(shared)
        per_core.append(m)
    return per_core


class _Sem:
    __slots__ = ("h", "n")

    def __init__(self, h):
        self.h = h
        self.n = 0


class _Prog:
    ENGS = ("tensor", "vector", "scalar", "sync", "gpsimd")

    def __init__(self):
        self.q = {e: [] for e in self.ENGS}

    def emit(self, eng, fn):
        self.q[eng].append(fn)

    def wait(self, eng, sem, n=None):
        if n is None:
            n = sem.n
        assert 0 <= n <= sem.n, f"wait {n} > scheduled {sem.n}"
        if n > 0:
            self.emit(eng, lambda e, h=sem.h, n=n: e.wait_ge(h, n))

    def run(self, eng, fn, inc=None, inc_n=1):
        if inc is not None:
            inc.n += inc_n

            def go(e, fn=fn, h=inc.h, k=inc_n):
                fn(e).then_inc(h, k)
        else:
            def go(e, fn=fn):
                fn(e)
        self.emit(eng, go)


def _rep0(base_ap, off, reps, width):
    pstep = base_ap.ap[0][0]
    return bass.AP(base_ap.tensor, base_ap.offset + off, [[pstep, 128], [0, reps], [1, width]])


def build_nc(debug=False):
    nc = bass.Bass("TRN2", target_bir_lowering=False, debug=False,
                   detect_race_conditions=False)
    XT = nc.dram_tensor("XT", [NTB, 128, KC * TB], F32, kind="ExternalInput").ap()
    WQKV = nc.dram_tensor("WQKV", [3 * KC, 128, KC * 128], F32, kind="ExternalInput").ap()
    BIN = nc.dram_tensor("BIN", [128, 3 * KC], F32, kind="ExternalInput").ap()
    WOUT = nc.dram_tensor("WOUT", [OC, 128, KC * 128], F32, kind="ExternalInput").ap()
    BOUT = nc.dram_tensor("BOUT", [128, OC], F32, kind="ExternalInput").ap()
    G = nc.dram_tensor("G", [128, KC], F32, kind="ExternalInput").ap()
    WGV = nc.dram_tensor("WGV", [GC, 128, KC * 128], F32, kind="ExternalInput").ap()
    WO2 = nc.dram_tensor("WO2", [OC, 128, HC * 128], F32, kind="ExternalInput").ap()
    COS = nc.dram_tensor("COS", [RC // 2, 128, S], F32, kind="ExternalInput").ap()
    SIN = nc.dram_tensor("SIN", [RC // 2, 128, S], F32, kind="ExternalInput").ap()
    CONSTS = nc.dram_tensor("CONSTS", [128, 4], F32, kind="ExternalInput").ap()
    ONESCOL = nc.dram_tensor("ONESCOL", [128, 1], F32, kind="ExternalInput").ap()
    ONESROW = nc.dram_tensor("ONESROW", [1, 128], F32, kind="ExternalInput").ap()
    IDENT = nc.dram_tensor("IDENT", [128, 128], F32, kind="ExternalInput").ap()
    OUT = nc.dram_tensor("OUT", [KC, 128, S], F32, kind="ExternalOutput").ap()
    ikind = "ExternalOutput" if debug else "Internal"
    TGV = nc.dram_tensor("TGV", [GC, 128, KC * 128], BF16, kind=ikind).ap()
    TO2 = nc.dram_tensor("TO2", [HC, 128, OC * 128], BF16, kind=ikind).ap()
    QBF = nc.dram_tensor("QBF", [NTB, 128, KC * TB], BF16, kind=ikind).ap()
    DBG = nc.dram_tensor("DBG", [128, 64], F32, kind="ExternalOutput").ap() if debug else None
    DBGY = nc.dram_tensor("DBGY", [128, 8192], F32, kind="ExternalOutput").ap() if debug else None
    DBGB = nc.dram_tensor("DBGB", [128, 1024], F32, kind="ExternalOutput").ap() if debug else None

    P = _Prog()
    es = contextlib.ExitStack()
    with es:
        arA = es.enter_context(nc.sbuf_tensor("arA", [128, 8192], F32))
        arB = es.enter_context(nc.sbuf_tensor("arB", [128, 6144], F32))
        arC = es.enter_context(nc.sbuf_tensor("arC", [128, 8192], F32))
        arD = es.enter_context(nc.sbuf_tensor("arD", [128, 6144], F32))
        arF = es.enter_context(nc.sbuf_tensor("arF", [128, 8192], F32))
        arG = es.enter_context(nc.sbuf_tensor("arG", [128, 8192], F32))
        wsl = es.enter_context(nc.sbuf_tensor("wsl", [128, 2048], F32))
        sml = es.enter_context(nc.sbuf_tensor("sml", [128, 176], F32))
        row = es.enter_context(nc.sbuf_tensor("rowbuf", [1, 3840], F32))
        idb = es.enter_context(nc.sbuf_tensor("idb", [128, 128], F32))
        ps = [es.enter_context(nc.psum_tensor(f"pb{i}", [128, 512], F32)) for i in range(8)]

        mksem = lambda name: _Sem(es.enter_context(nc.semaphore(name)))
        dPL = mksem("dPL")
        dSL = [mksem("dSL0"), mksem("dSL1")]
        dSLg = [mksem("dSLg0"), mksem("dSLg1")]
        dO2 = [mksem("dO20"), mksem("dO21")]
        dXT = mksem("dXT")
        dCOS = mksem("dCOS")
        dSIN = mksem("dSIN")
        dXB = mksem("dXB")
        dQB = mksem("dQB")
        dQS = mksem("dQS")
        dTS = [mksem("dTS0"), mksem("dTS1")]
        dOS = [mksem("dOS0"), mksem("dOS1")]
        dDB = mksem("dDB")
        gG = mksem("gG")
        tT = mksem("tT")
        vV = mksem("vV")
        sS = mksem("sS")
        block = es.enter_context(nc.Block())

        CT = sml[:, 0:4]
        ONEC = sml[:, 4:5]
        ONECB = sml[:, 5:6].bitcast(BF16)[:, 0:1]
        BINc = sml[:, 8:56]
        BOUTc = sml[:, 56:72]
        Gc = sml[:, 72:88]
        Acol = sml[:, 88:104]
        FLAG = sml[:, 104:105]
        RSGV = sml[:, 105:106]
        RSO2 = sml[:, 106:107]
        ONER = row[0:1, 0:128]
        SWGV = row[0:1, 132:133]
        SWO2 = row[0:1, 133:134]
        for v, c in ((1.0, 0), (EPS, 1), (-0.5, 2), (0.5, 3)):
            nc.const_aps.register(F32, v, CT[:, c:c + 1])

        for dst, src in ((sml[:, 0:4], CONSTS), (sml[:, 4:5], ONESCOL),
                         (row[0:1, 0:128], ONESROW), (idb[:], IDENT),
                         (BINc, BIN), (BOUTc, BOUT), (Gc, G)):
            P.run("sync", lambda e, dst=dst, src=src: e.dma_start(out=dst, in_=src),
                  inc=dPL, inc_n=16)
        P.wait("vector", dPL)
        P.run("vector", lambda e: e.tensor_scalar(FLAG, ONEC, M64, M64, ALU.add, ALU.subtract))
        P.run("vector", lambda e: e.tensor_copy(ONECB, ONEC))
        P.run("vector", lambda e: e.memset(Acol, 0.0), inc=vV)

        bank_tok = [None] * 8

        def bank_wait(bank):
            t = bank_tok[bank]
            if t is not None:
                P.wait("tensor", t[0], t[1])

        def bank_read(bank, sem):
            bank_tok[bank] = (sem, sem.n)

        arC_bfq = arC[:].bitcast(BF16)
        arG_bfq = arG[:].bitcast(BF16)
        arB_bfq = arB[:].bitcast(BF16)
        arD_bfq = arD[:].bitcast(BF16)
        qp_in = [arF[:, 0:8192], arA[:, 0:8192]]
        qp_absl = [arC_bfq[:, 0:8192], arC_bfq[:, 8192:16384]]
        qp_t1 = [arG_bfq[:, 0:8192], arC_bfq[:, 0:8192]]
        qp_t2 = [arG_bfq[:, 8192:16384], arC_bfq[:, 8192:16384]]
        qp_out = [arD_bfq[:, 0:8192], arB_bfq[:, 0:8192]]

        def _quad(img, i, nsl=4):
            w = img.shape[2]
            return bass.AP(img.tensor, img.offset + i * nsl * 128 * w,
                           [[w, 128], [128 * w, nsl], [1, w]])

        def qp_sum_pass(img, nunit, psbank):
            mm_of = {}
            abs_of = {}
            P.wait("sync", sS)
            P.wait("scalar", tT)
            for i in range(nunit):
                b = i % 2
                if i >= 2:
                    P.wait("sync", sS, abs_of[i - 2])
                P.run("sync", lambda e, i=i, b=b, img=img: e.dma_start(out=qp_in[b], in_=img(i)),
                      inc=dSL[b], inc_n=16)
                P.wait("scalar", dSL[b])
                if i >= 2:
                    P.wait("scalar", tT, mm_of[i - 2])
                P.run("scalar", lambda e, b=b: e.activation(qp_absl[b], qp_in[b], AF.Abs,
                                                            bias=0.0, scale=1.0), inc=sS)
                abs_of[i] = sS.n
                P.wait("tensor", sS)
                for j in range(16):
                    P.run("tensor", lambda e, b=b, j=j, i=i, psbank=psbank, nunit=nunit:
                          e.matmul(ps[psbank][0:1, 0:512], ONECB,
                                   qp_absl[b][:, 512 * j:512 * (j + 1)],
                                   start=(i == 0 and j == 0),
                                   stop=(i == nunit - 1 and j == 15)),
                          inc=(tT if j == 15 else None))
                mm_of[i] = tT.n

        def qp_finish(psbank, recip_col, srow_cell, count):
            P.wait("vector", tT)
            r0 = row[0:1, 136 + psbank:137 + psbank]
            P.run("vector", lambda e: e.reduce_sum(r0, ps[psbank][0:1, 0:512], axis=AX.X),
                  inc=vV)
            P.wait("vector", vV)
            P.run("vector", lambda e: e.tensor_scalar(srow_cell, r0, 1.0 / count, 1e-5,
                                                      ALU.mult, ALU.max), inc=vV)
            P.wait("vector", vV)
            rc = row[0:1, 140 + psbank:141 + psbank]
            P.run("vector", lambda e: e.reciprocal(rc, srow_cell), inc=vV)
            P.wait("tensor", vV)
            bank_wait(5)
            P.run("tensor", lambda e: e.matmul(ps[5][:, 0:1], ONER, rc, start=True, stop=True),
                  inc=tT)
            P.wait("vector", tT)
            P.run("vector", lambda e: e.tensor_copy(recip_col, ps[5][:, 0:1]), inc=vV)
            bank_read(5, vV)

        tn_in = [arF[:, 0:4096], arF[:, 4096:8192]]
        arG_bft = arG[:].bitcast(BF16)
        arD_bft = arD[:].bitcast(BF16)
        tn_t1 = [arG_bft[:, 0:4096], arG_bft[:, 8192:12288]]
        tn_t2 = [arG_bft[:, 4096:8192], arG_bft[:, 12288:16384]]
        tn_out = [arD_bft[:, 8192:12288], arD_bft[:, 8192:12288]]


        gv_in2 = lambda i: _quad(WGV, i, 2)
        gv_out2 = lambda i: _quad(TGV, i, 2)
        o2_in2 = lambda i: WO2[i // 2][:, 4096 * (i % 2):4096 * (i % 2 + 1)]

        def o2_out2(i):
            oc, kh = i // 2, i % 2
            return bass.AP(TO2.tensor, TO2.offset + 32 * kh * (128 * OC * 128) + 128 * oc,
                           [[OC * 128, 128], [128 * OC * 128, 32], [1, 128]])
        TN_UNITS = [(gv_in2, gv_out2, GC // 2, RSGV), (o2_in2, o2_out2, 2 * OC, RSO2)]
        tn_state = {"pass": 0, "i": 0, "sgn": {}, "add": {}}

        def tern_step():
            st = tn_state
            if st["pass"] >= len(TN_UNITS):
                return
            img, out_img, nunit, recip_col = TN_UNITS[st["pass"]]
            i, b = st["i"], st["i"] % 2
            key = (st["pass"], i - 2)
            use_v = False
            if key in st["sgn"]:
                P.wait("sync", sS, st["sgn"][key][0])
                P.wait("sync", vV, st["sgn"][key][1])
            P.run("sync", lambda e, i=i, b=b, img=img: e.dma_start(out=tn_in[b], in_=img(i)),
                  inc=dSL[b], inc_n=16)
            if use_v:
                P.wait("vector", dSL[b])
                if key in st["add"]:
                    P.wait("vector", vV, st["add"][key])
                P.run("vector", lambda e, b=b, recip_col=recip_col: e.tensor_scalar(
                    tn_t1[b], tn_in[b], recip_col, 0.5, ALU.mult, ALU.is_ge))
                P.run("vector", lambda e, b=b, recip_col=recip_col: e.tensor_scalar(
                    tn_t2[b], tn_in[b], recip_col, -0.5, ALU.mult, ALU.is_le), inc=vV)
                st["sgn"][(st["pass"], i)] = (sS.n, vV.n)
                P.wait("vector", dTS[0])
                P.run("vector", lambda e, b=b: e.tensor_tensor(
                    tn_t1[b], tn_t1[b], tn_t2[b], ALU.subtract))
                P.run("vector", lambda e, b=b: e.tensor_scalar(
                    tn_out[b], tn_t1[b], 2.0, None, ALU.mult), inc=vV)
            else:
                P.wait("scalar", dSL[b])
                if key in st["add"]:
                    P.wait("scalar", vV, st["add"][key])
                P.run("scalar", lambda e, b=b, recip_col=recip_col: e.activation(
                    tn_t1[b], tn_in[b], AF.Sign, bias=-0.5, scale=recip_col))
                P.run("scalar", lambda e, b=b, recip_col=recip_col: e.activation(
                    tn_t2[b], tn_in[b], AF.Sign, bias=0.5, scale=recip_col), inc=sS)
                st["sgn"][(st["pass"], i)] = (sS.n, vV.n)
                P.wait("vector", sS)
                P.wait("vector", dTS[0])
                P.run("vector", lambda e, b=b: e.tensor_tensor(tn_out[b], tn_t1[b], tn_t2[b],
                                                               ALU.add), inc=vV)
            st["add"][(st["pass"], i)] = vV.n
            P.wait("sync", vV)
            P.run("sync", lambda e, i=i, b=b, out_img=out_img:
                  e.dma_start(out=out_img(i), in_=tn_out[b]), inc=dTS[0], inc_n=16)
            st["i"] += 1
            if st["i"] >= nunit:
                st["pass"] += 1
                st["i"] = 0

        def tern_flush():
            while tn_state["pass"] < len(TN_UNITS):
                tern_step()

        P.wait("scalar", dPL)
        P.wait("tensor", vV)
        gv_in = lambda i: _quad(WGV, i)
        o2_in = lambda i: WO2[i]
        qp_sum_pass(gv_in, GC // 4, 2)
        qp_finish(2, RSGV, SWGV, GC * 128 * 2048)
        qp_sum_pass(o2_in, OC, 3)
        qp_finish(3, RSO2, SWO2, OC * 128 * 8192)
        P.wait("gpsimd", sS)
        P.wait("gpsimd", vV)

        arC_bf = arC[:].bitcast(BF16)
        arA_bf = arA[:].bitcast(BF16)
        wslb = wsl[:].bitcast(BF16)
        slab = [wslb[:, 0:2048], wslb[:, 2048:4096]]
        slab_cons = [0, 0]
        slab_sem = [dSL[0], dSL[1]]

        def load_slab(img, c, si, eng="gpsimd"):
            b = si % 2
            if slab_cons[b] > 0:
                P.wait(eng, tT, slab_cons[b])
            sem = dSLg[b] if eng == "gpsimd" else dSL[b]
            slab_sem[b] = sem
            P.run(eng, lambda e, img=img, c=c, b=b: e.dma_start(out=slab[b], in_=img[c]),
                  inc=sem, inc_n=16)
            return b

        def mm_chunk(b, rhs_bf, psbank, extra_wait=None):
            P.wait("tensor", slab_sem[b])
            bank_wait(psbank)
            if extra_wait is not None:
                P.wait("tensor", extra_wait)
            rhs_of = rhs_bf if callable(rhs_bf) else (
                lambda k, r=rhs_bf: r[:, TB * k:TB * (k + 1)])
            for k in range(KC):
                P.run("tensor", lambda e, b=b, k=k, psbank=psbank, rhs_of=rhs_of:
                      e.matmul(ps[psbank][:, 0:TB], slab[b][:, 128 * k:128 * (k + 1)],
                               rhs_of(k), start=(k == 0), stop=(k == KC - 1)),
                      inc=(tT if k == KC - 1 else None))

        XTC = arC_bf

        def xt_tile(tl, k):
            return XTC[:, 8192 * tl + 512 * k:8192 * tl + 512 * (k + 1)]

        def mm_group(b, tl, bank):
            P.wait("tensor", slab_sem[b])
            bank_wait(bank)
            for k in range(KC):
                P.run("tensor", lambda e, b=b, tl=tl, k=k, bank=bank:
                      e.matmul(ps[bank][:, 0:TB], slab[b][:, 128 * k:128 * (k + 1)],
                               xt_tile(tl, k), start=(k == 0), stop=(k == KC - 1)),
                      inc=(tT if k == KC - 1 else None))

        arD_bf = arD[:].bitcast(BF16)
        COSb = arD_bf[:, 0:1024]
        SINb = arD_bf[:, 1024:2048]
        rBt = [arD[:, 2048 + 512 * t:2048 + 512 * (t + 1)] for t in range(NTB)]
        si = [0]
        HW_ = 1024

        KB0 = arB[:, 0:HW_]
        KB1 = arB[:, HW_:2 * HW_]
        VST = arA[:, 0:HW_]
        TM1 = arA[:, 2048:2048 + HW_]
        TM2 = arA[:, 4096:4096 + HW_]
        PBUF = arA[:, 6144:6144 + HW_]
        SRED2 = sml[:, 108:110]
        SRED1 = sml[:, 112:113]
        QEb = arA_bf[:, 0:HW_]
        QOb = arA_bf[:, HW_:2 * HW_]
        QPb = arA_bf[:, 2 * HW_:3 * HW_]
        sqb = [arA[:, 512 * t:512 * (t + 1)] for t in range(2)]

        for hf in range(2):
            gtb = (2 * hf, 2 * hf + 1)
            if hf > 0:
                P.wait("gpsimd", tT)
            for tl in range(2):
                P.run("gpsimd", lambda e, hf=hf, tl=tl: e.dma_start(
                    out=XTC[:, 8192 * tl:8192 * (tl + 1)], in_=XT[2 * hf + tl]),
                    inc=dXT, inc_n=16)

            statmm = {}
            P.wait("tensor", dXT)
            for c in range(KC):
                b = load_slab(WQKV, KC + c, si[0]); si[0] += 1
                for tl in range(2):
                    mm_group(b, tl, tl)
                    if tl == 1:
                        slab_cons[b] = tT.n
                    P.wait("scalar", tT)
                    if (c - 1, tl) in statmm:
                        P.wait("scalar", tT, statmm[(c - 1, tl)])
                    P.run("scalar", lambda e, tl=tl, c=c: e.activation(
                        sqb[tl], ps[tl][:, 0:TB], AF.Square,
                        bias=BINc[:, KC + c:KC + c + 1], scale=1.0), inc=sS)
                    bank_read(tl, sS)
                    P.wait("tensor", sS)
                    if c == 0:
                        bank_wait(4 + tl)
                    P.run("tensor", lambda e, c=c, tl=tl: e.matmul(
                        ps[4 + tl][0:1, 0:TB], ONEC, sqb[tl],
                        start=(c == 0), stop=(c == KC - 1)), inc=tT)
                    statmm[(c, tl)] = tT.n
                tern_step()
                tern_step()
            P.wait("scalar", tT)
            for tl in range(2):
                RT0 = row[0:1, 256 + 512 * tl:256 + 512 * (tl + 1)]
                P.run("scalar", lambda e, RT0=RT0, tl=tl: e.activation(
                    RT0, ps[4 + tl][0:1, 0:TB], AF.Sqrt, bias=0.0, scale=1.0), inc=sS)
                bank_read(4 + tl, sS)
            P.wait("vector", sS)
            for tl in range(2):
                RT0 = row[0:1, 256 + 512 * tl:256 + 512 * (tl + 1)]
                P.run("vector", lambda e, RT0=RT0: e.tensor_scalar(RT0, RT0, 1e-12, None,
                                                                   ALU.max))
                P.run("vector", lambda e, RT0=RT0: e.reciprocal(RT0, RT0),
                      inc=(vV if tl == 1 else None))
            P.wait("tensor", vV)
            for tl in range(2):
                RT0 = row[0:1, 256 + 512 * tl:256 + 512 * (tl + 1)]
                bank_wait(tl)
                P.run("tensor", lambda e, RT0=RT0, tl=tl: e.matmul(
                    ps[tl][:, 0:TB], ONER, RT0, start=True, stop=True), inc=tT)
            P.wait("scalar", tT)
            for tl in range(2):
                P.run("scalar", lambda e, tl=tl, hf=hf: e.copy(rBt[2 * hf + tl],
                                                               ps[tl][:, 0:TB]), inc=sS)
                bank_read(tl, sS)

            kb_read = [0, 0]
            rope_done = 0

            def proj_chunk(wcol, dstbuf, bias_col, read_guard=None):
                b = load_slab(WQKV, wcol, si[0]); si[0] += 1
                for tl in range(2):
                    mm_group(b, tl, tl)
                    if tl == 1:
                        slab_cons[b] = tT.n
                    P.wait("vector", tT)
                    if tl == 0 and read_guard:
                        P.wait("vector", vV, read_guard)
                    P.run("vector", lambda e, tl=tl, dstbuf=dstbuf, bias_col=bias_col:
                          e.tensor_scalar(dstbuf[:, 512 * tl:512 * (tl + 1)],
                                          ps[tl][:, 0:TB], bias_col, None, ALU.add),
                          inc=vV)
                    bank_read(tl, vV)
                tern_step()

            def rope_pair(E, O, outE=None, outO=None):
                P.run("vector", lambda e, O=O: e.tensor_tensor(TM1, O, SINb, ALU.mult))
                P.run("vector", lambda e, E=E: e.tensor_tensor(TM2, E, SINb, ALU.mult))
                P.run("vector", lambda e, E=E: e.tensor_tensor(E, E, COSb, ALU.mult))
                P.run("vector", lambda e, E=E, outE=outE: e.tensor_tensor(
                    outE if outE is not None else E, E, TM1, ALU.subtract))
                P.run("vector", lambda e, O=O: e.tensor_tensor(O, O, COSb, ALU.mult))
                P.run("vector", lambda e, O=O, outO=outO: e.tensor_tensor(
                    outO if outO is not None else O, O, TM2, ALU.add), inc=vV)

            def a_accum(krbuf, cv):
                P.run("vector", lambda e, krbuf=krbuf: e.tensor_tensor(PBUF, krbuf, VST,
                                                                       ALU.mult))
                for tl in range(2):
                    P.run("vector", lambda e, tl=tl, hf=hf: e.tensor_tensor(
                        PBUF[:, 512 * tl:512 * (tl + 1)], PBUF[:, 512 * tl:512 * (tl + 1)],
                        rBt[2 * hf + tl], ALU.mult))
                p3 = bass.AP(arA[:].tensor, arA[:].offset + 6144,
                             [[8192, 128], [512, 2], [1, 512]])
                P.run("vector", lambda e, p3=p3: e.reduce_sum(SRED2, p3, axis=AX.X), inc=vV)
                P.wait("vector", vV)
                P.run("vector", lambda e: e.reduce_sum(SRED1, SRED2, axis=AX.X), inc=vV)
                P.wait("vector", vV)
                P.run("vector", lambda e, cv=cv: e.tensor_tensor(
                    Acol[:, cv:cv + 1], Acol[:, cv:cv + 1], SRED1, ALU.add), inc=vV)

            for pb in range(RC // 2):
                c0, c1 = 2 * pb, 2 * pb + 1
                P.wait("gpsimd", vV, rope_done)
                P.run("gpsimd", lambda e, pb=pb, hf=hf: e.dma_start(
                    out=COSb, in_=COS[pb][:, HW_ * hf:HW_ * (hf + 1)]), inc=dCOS, inc_n=16)
                P.run("gpsimd", lambda e, pb=pb, hf=hf: e.dma_start(
                    out=SINb, in_=SIN[pb][:, HW_ * hf:HW_ * (hf + 1)]), inc=dSIN, inc_n=16)
                proj_chunk(KC + c0, KB0, BINc[:, KC + c0:KC + c0 + 1], read_guard=kb_read[0])
                proj_chunk(KC + c1, KB1, BINc[:, KC + c1:KC + c1 + 1], read_guard=kb_read[1])
                P.wait("vector", dCOS)
                P.wait("vector", dSIN)
                rope_pair(KB0, KB1)
                rope_done = vV.n
                proj_chunk(2 * KC + c0, VST, BINc[:, 2 * KC + c0:2 * KC + c0 + 1],
                           read_guard=vV.n)
                a_accum(KB0, c0)
                proj_chunk(2 * KC + c1, VST, BINc[:, 2 * KC + c1:2 * KC + c1 + 1],
                           read_guard=vV.n)
                a_accum(KB1, c1)
                kb_read = [vV.n, vV.n]
            for c in range(RC, KC):
                proj_chunk(KC + c, KB0, BINc[:, KC + c:KC + c + 1], read_guard=kb_read[0])
                proj_chunk(2 * KC + c, VST, BINc[:, 2 * KC + c:2 * KC + c + 1],
                           read_guard=vV.n)
                a_accum(KB0, c)
                kb_read[0] = vV.n

            def q_spill_chunk(src_bf, c):
                for tl in range(2):
                    P.run("sync", lambda e, tl=tl, c=c, hf=hf, src_bf=src_bf: e.dma_start(
                        out=QBF[2 * hf + tl][:, 512 * c:512 * (c + 1)],
                        in_=src_bf[:, 512 * tl:512 * (tl + 1)]), inc=dQS, inc_n=16)

            q_guard = kb_read[0]
            for pb in range(RC // 2):
                c0, c1 = 2 * pb, 2 * pb + 1
                P.wait("gpsimd", vV, rope_done)
                P.run("gpsimd", lambda e, pb=pb, hf=hf: e.dma_start(
                    out=COSb, in_=COS[pb][:, HW_ * hf:HW_ * (hf + 1)]), inc=dCOS, inc_n=16)
                P.run("gpsimd", lambda e, pb=pb, hf=hf: e.dma_start(
                    out=SINb, in_=SIN[pb][:, HW_ * hf:HW_ * (hf + 1)]), inc=dSIN, inc_n=16)
                proj_chunk(c0, KB0, BINc[:, c0:c0 + 1], read_guard=q_guard)
                proj_chunk(c1, KB1, BINc[:, c1:c1 + 1])
                P.wait("vector", dCOS)
                P.wait("vector", dSIN)
                P.wait("vector", dQS)
                rope_pair(KB0, KB1, outE=QEb, outO=QOb)
                rope_done = vV.n
                q_guard = vV.n
                P.wait("sync", vV)
                q_spill_chunk(QEb, c0)
                q_spill_chunk(QOb, c1)
            for c in range(RC, KC):
                b = load_slab(WQKV, c, si[0]); si[0] += 1
                for tl in range(2):
                    mm_group(b, tl, tl)
                    if tl == 1:
                        slab_cons[b] = tT.n
                    P.wait("vector", tT)
                    if tl == 0:
                        P.wait("vector", dQS)
                    P.run("vector", lambda e, tl=tl, c=c: e.tensor_scalar(
                        QPb[:, 512 * tl:512 * (tl + 1)], ps[tl][:, 0:TB],
                        BINc[:, c:c + 1], None, ALU.add), inc=vV)
                    bank_read(tl, vV)
                tern_step()
                P.wait("sync", vV)
                q_spill_chunk(QPb, c)

        tern_flush()
        ph1_t, ph1_v, ph1_s = tT.n, vV.n, sS.n

        arG_bf = arG[:].bitcast(BF16)
        Qbf = arG_bf[:, 0:KC * TB]
        yslot = [arG_bf[:, KC * TB:KC * TB + TB]] + \
                [arG_bf[:, TB * c:TB * (c + 1)] for c in range(KC - 1)]
        Xbf2 = arC_bf[:, 0:KC * TB]
        YB = arA
        X1 = arC
        arD_bf2 = arD[:].bitcast(BF16)
        Q1 = arD_bf2[:, 4096:4096 + KC * TB]
        arF_bf2 = arF[:].bitcast(BF16)
        wslot = [arA[:].bitcast(BF16)[:, TB * i:TB * (i + 1)] for i in range(32)] + \
                [arF_bf2[:, TB * i:TB * (i + 1)] for i in range(32)]
        o2sl = [arG_bf[:, 0:HC * 128], arG_bf[:, HC * 128:2 * HC * 128]]
        TMP = [arB[:, 0:512], arB[:, 512:1024]]
        WMXr = arB[:, 3072:3072 + TB]
        WMNr = arB[:, 3584:3584 + TB]

        RM = row[0:1, 256:256 + TB]
        RMSQ = row[0:1, 768:768 + TB]
        RVAR = row[0:1, 1280:1280 + TB]
        RSTD = row[0:1, 1792:1792 + TB]
        RB1 = row[0:1, 2304:2304 + TB]
        RSTDr = row[0:1, 2816:2816 + TB]
        MROW = row[0:1, 3328:3328 + TB]
        F1B = arD[:, 0:TB]
        B1B = arD[:, TB:2 * TB]
        PSIB = arD[:, 2 * TB:3 * TB]
        ALPB = arD[:, 3 * TB:4 * TB]

        def bcast_row(src_row, dst, bank):
            P.wait("tensor", vV)
            P.wait("tensor", sS)
            bank_wait(bank)
            P.run("tensor", lambda e, src_row=src_row, bank=bank: e.matmul(
                ps[bank][:, 0:TB], ONER, src_row, start=True, stop=True), inc=tT)
            P.wait("scalar", tT)
            P.run("scalar", lambda e, dst=dst, bank=bank: e.copy(dst, ps[bank][:, 0:TB]),
                  inc=sS)
            bank_read(bank, sS)

        def colmax_to_row(src_tile, dst_row):
            P.wait("tensor", vV)
            bank_wait(5)
            for j in range(TB // 128):
                P.run("tensor", lambda e, j=j: e.transpose(
                    ps[5][:, 128 * j:128 * (j + 1)], src_tile[:, 128 * j:128 * (j + 1)],
                    idb[:]), inc=(tT if j == TB // 128 - 1 else None))
            P.wait("vector", tT)
            for j in range(TB // 128):
                P.run("vector", lambda e, j=j: e.reduce_max(
                    arB[:, 4096 + 2 * j:4096 + 2 * j + 1], ps[5][:, 128 * j:128 * (j + 1)],
                    axis=AX.X), inc=(vV if j == TB // 128 - 1 else None))
            bank_read(5, vV)
            P.wait("tensor", vV)
            bank_wait(6)
            for j in range(TB // 128):
                P.run("tensor", lambda e, j=j: e.transpose(
                    ps[6][0:1, 128 * j:128 * (j + 1)], arB[:, 4096 + 2 * j:4096 + 2 * j + 1],
                    idb[:]), inc=(tT if j == TB // 128 - 1 else None))
            P.wait("scalar", tT)
            P.run("scalar", lambda e, dst_row=dst_row: e.copy(dst_row, ps[6][0:1, 0:TB]),
                  inc=sS)
            bank_read(6, sS)

        for tb in range(NTB):
            if tb == 0:
                P.wait("sync", tT, ph1_t)
                P.wait("sync", vV, ph1_v)
                P.wait("sync", dQS)
                P.wait("sync", dTS[0])
                P.wait("sync", dTS[1])
                P.wait("gpsimd", tT, ph1_t)
                P.wait("gpsimd", vV, ph1_v)
            else:
                P.wait("sync", tT)
                P.wait("sync", vV)
                P.wait("gpsimd", vV)
            P.run("sync", lambda e, tb=tb: e.dma_start(out=Qbf, in_=QBF[tb]), inc=dQB, inc_n=16)
            P.run("gpsimd", lambda e, tb=tb: e.dma_start(out=Xbf2, in_=XT[tb]), inc=dXB, inc_n=16)

            P.wait("vector", dQB)
            for c in range(KC):
                P.run("vector", lambda e, c=c: e.tensor_scalar(
                    yslot[c], Qbf[:, TB * c:TB * (c + 1)],
                    Acol[:, c:c + 1], None, ALU.mult), inc=(vV if c == KC - 1 else None))

            for c in range(OC):
                b = load_slab(WOUT, c, si[0]); si[0] += 1
                mm_chunk(b, (lambda k: yslot[k]), c % 4, extra_wait=(vV if c == 0 else None))
                slab_cons[b] = tT.n
                P.wait("scalar", tT)
                if c == 0:
                    P.wait("scalar", vV)
                    P.wait("scalar", tT)
                P.run("scalar", lambda e, c=c: e.activation(
                    YB[:, TB * c:TB * (c + 1)], ps[c % 4][:, 0:TB], AF.Identity,
                    bias=BOUTc[:, c:c + 1], scale=1.0), inc=sS)
                bank_read(c % 4, sS)
            P.wait("vector", sS)
            P.wait("vector", dXB)
            P.run("vector", lambda e: e.scalar_tensor_tensor(
                YB[:, 0:KC * TB], Xbf2, 2.0, YB[:, 0:KC * TB], ALU.mult, ALU.add), inc=vV)
            sq = [arB[:, 1024:1024 + TB], arB[:, 1536:1536 + TB]]
            sqmm = {}
            P.wait("scalar", vV)
            for c in range(KC):
                bq = c % 2
                if c >= 2:
                    P.wait("scalar", tT, sqmm[c - 2])
                P.run("scalar", lambda e, c=c, bq=bq: e.activation(
                    sq[bq], YB[:, TB * c:TB * (c + 1)], AF.Square, bias=0.0, scale=1.0),
                    inc=sS)
                P.wait("tensor", sS)
                if c == 0:
                    bank_wait(2)
                    bank_wait(3)
                P.run("tensor", lambda e, c=c: e.matmul(
                    ps[2][0:1, 0:TB], ONEC, YB[:, TB * c:TB * (c + 1)],
                    start=(c == 0), stop=(c == KC - 1)))
                P.run("tensor", lambda e, c=c, bq=bq: e.matmul(
                    ps[3][0:1, 0:TB], ONEC, sq[bq],
                    start=(c == 0), stop=(c == KC - 1)), inc=tT)
                sqmm[c] = tT.n
            P.wait("scalar", tT)
            P.run("scalar", lambda e: e.activation(RM, ps[2][0:1, 0:TB], AF.Identity,
                                                   bias=0.0, scale=1.0 / D))
            P.run("scalar", lambda e: e.activation(RMSQ, ps[3][0:1, 0:TB], AF.Identity,
                                                   bias=0.0, scale=1.0 / D), inc=sS)
            bank_read(2, sS)
            bank_read(3, sS)
            P.wait("vector", sS)
            P.run("vector", lambda e: e.tensor_tensor(RVAR, RM, RM, ALU.mult))
            P.run("vector", lambda e: e.tensor_tensor(RVAR, RMSQ, RVAR, ALU.subtract), inc=vV)
            P.wait("scalar", vV)
            P.run("scalar", lambda e: e.activation(RSTD, RVAR, AF.Sqrt, bias=EPS, scale=1.0),
                  inc=sS)
            P.wait("vector", sS)
            P.run("vector", lambda e: e.reciprocal(RSTDr, RSTD))
            P.run("vector", lambda e: e.tensor_tensor(RB1, RM, RSTDr, ALU.mult), inc=vV)
            bcast_row(RSTDr, F1B, 4)
            bcast_row(RB1, B1B, 5)
            P.wait("vector", sS)
            f1rep = _rep0(arD[:], 0, KC, TB)
            b1rep = _rep0(arD[:], TB, KC, TB)
            P.run("vector", lambda e, f1rep=f1rep: e.tensor_tensor(
                X1[:, 0:KC * TB], YB[:, 0:KC * TB], f1rep, ALU.mult))
            P.run("vector", lambda e, b1rep=b1rep: e.tensor_tensor(
                X1[:, 0:KC * TB], X1[:, 0:KC * TB], b1rep, ALU.subtract), inc=vV)

            for c in range(KC):
                P.run("vector", lambda e, c=c: e.tensor_scalar(
                    YB[:, TB * c:TB * (c + 1)], X1[:, TB * c:TB * (c + 1)],
                    Gc[:, c:c + 1], None, ALU.mult), inc=(vV if c == KC - 1 else None))
            MX = arB[:, 2048:2048 + TB]
            MN = arB[:, 2560:2560 + TB]
            P.run("vector", lambda e: e.tensor_tensor(MX, YB[:, 0:TB], YB[:, TB:2 * TB], ALU.max))
            P.run("vector", lambda e: e.tensor_tensor(MN, YB[:, 0:TB], YB[:, TB:2 * TB], ALU.min))
            for c in range(2, KC):
                P.run("vector", lambda e, c=c: e.tensor_tensor(
                    MX, MX, YB[:, TB * c:TB * (c + 1)], ALU.max))
                P.run("vector", lambda e, c=c: e.tensor_tensor(
                    MN, MN, YB[:, TB * c:TB * (c + 1)], ALU.min))
            P.run("vector", lambda e: e.tensor_scalar(MN, MN, -1.0, None, ALU.mult))
            P.run("vector", lambda e: e.tensor_tensor(MX, MX, MN, ALU.max), inc=vV)
            colmax_to_row(MX, MROW)
            RD = RSTD
            RDr = RB1
            P.wait("vector", sS)
            P.run("vector", lambda e: e.tensor_scalar(RD, RVAR, EPS, None, ALU.add))
            P.run("vector", lambda e: e.reciprocal(RDr, RD))
            P.run("vector", lambda e: e.tensor_tensor(RD, RVAR, RDr, ALU.mult), inc=vV)
            P.wait("scalar", vV)
            P.run("scalar", lambda e: e.activation(RDr, RD, AF.Sqrt, bias=EPS, scale=1.0),
                  inc=sS)
            P.wait("vector", sS)
            P.run("vector", lambda e: e.reciprocal(RD, RDr))
            RMAXH = RMSQ
            P.run("vector", lambda e: e.tensor_tensor(RMAXH, RD, MROW, ALU.mult))
            P.run("vector", lambda e: e.tensor_scalar(RMAXH, RMAXH, 1e-5, None, ALU.max))
            RS1 = RM
            P.run("vector", lambda e: e.reciprocal(RS1, RMAXH))
            P.run("vector", lambda e: e.tensor_scalar(RS1, RS1, QB, None, ALU.mult))
            RPSI = RVAR
            P.run("vector", lambda e: e.tensor_tensor(RPSI, RD, RS1, ALU.mult))
            RALPHA = RSTDr
            P.run("vector", lambda e: e.tensor_scalar(
                RALPHA, RMAXH, SWGV, 0.5 / QB, ALU.mult, ALU.mult), inc=vV)
            bcast_row(RPSI, PSIB, 4)
            bcast_row(RALPHA, ALPB, 5)
            P.wait("vector", sS)
            psirep = _rep0(arD[:], 2 * TB, KC, TB)
            P.run("vector", lambda e, psirep=psirep: e.tensor_tensor(
                YB[:, 0:KC * TB], YB[:, 0:KC * TB], psirep, ALU.mult), inc=vV)
            for c in range(KC):
                zc = YB[:, TB * c:TB * (c + 1)]
                P.run("vector", lambda e, zc=zc: e.tensor_scalar(
                    TMP[0][:, 0:TB], zc, M32, M32, ALU.add, ALU.subtract))
                P.run("vector", lambda e, zc=zc: e.tensor_scalar(
                    TMP[1][:, 0:TB], zc, M64, M64, ALU.add, ALU.subtract))
                P.run("vector", lambda e: e.tensor_tensor(
                    TMP[1][:, 0:TB], TMP[1][:, 0:TB], TMP[0][:, 0:TB], ALU.subtract))
                P.run("vector", lambda e, c=c: e.scalar_tensor_tensor(
                    Q1[:, TB * c:TB * (c + 1)], TMP[1][:, 0:TB], FLAG[:, 0:1],
                    TMP[0][:, 0:TB], ALU.mult, ALU.add), inc=(vV if c == KC - 1 else None))

            WMX = WMXr
            WMN = WMNr
            P.run("vector", lambda e: e.memset(WMX, 0.0))
            P.run("vector", lambda e: e.memset(WMN, 0.0), inc=vV)
            gvg = [arB[:, 4608:4608 + TB], arB[:, 5120:5120 + TB]]
            gvs = [TMP[0][:, 0:TB], TMP[1][:, 0:TB]]
            for h in range(HC):
                par = h % 2
                vbank, gbank = (6, 7) if par == 0 else (4, 5)
                bv_ = load_slab(TGV, 2 * h, si[0], eng="sync"); si[0] += 1
                mm_chunk(bv_, Q1, vbank, extra_wait=(vV if h == 0 else None))
                slab_cons[bv_] = tT.n
                bg_ = load_slab(TGV, 2 * h + 1, si[0], eng="sync"); si[0] += 1
                mm_chunk(bg_, Q1, gbank)
                slab_cons[bg_] = tT.n
                P.wait("vector", tT)
                P.run("vector", lambda e, par=par, gbank=gbank: e.tensor_tensor(
                    gvg[par], ps[gbank][:, 0:TB], ALPB[:, 0:TB], ALU.mult), inc=vV)
                P.wait("scalar", vV)
                P.run("scalar", lambda e, par=par: e.activation(gvs[par], gvg[par],
                                                                AF.Sigmoid, bias=0.0, scale=1.0),
                      inc=sS)
                P.wait("vector", sS)
                P.run("vector", lambda e, par=par, vbank=vbank: e.tensor_tensor(
                    gvs[par], ps[vbank][:, 0:TB], gvs[par], ALU.mult))
                P.run("vector", lambda e, h=h, par=par: e.tensor_tensor(
                    wslot[h], gvs[par], gvg[par], ALU.mult))
                P.run("vector", lambda e, h=h: e.tensor_tensor(
                    WMX, WMX, wslot[h], ALU.max))
                P.run("vector", lambda e, h=h: e.tensor_tensor(
                    WMN, WMN, wslot[h], ALU.min), inc=vV)
                bank_read(vbank, vV)
                bank_read(gbank, vV)
            P.run("vector", lambda e: e.tensor_scalar(WMN, WMN, -1.0, None, ALU.mult))
            P.run("vector", lambda e: e.tensor_tensor(WMX, WMX, WMN, ALU.max), inc=vV)
            WROW = MROW
            colmax_to_row(WMX, WROW)
            RMAXU = RM
            P.wait("vector", sS)
            P.run("vector", lambda e: e.tensor_tensor(RMAXU, RALPHA, WROW, ALU.mult))
            P.run("vector", lambda e: e.tensor_scalar(RMAXU, RMAXU, 1e-5, None, ALU.max))
            RS2 = RMSQ
            P.run("vector", lambda e: e.reciprocal(RS2, RMAXU))
            P.run("vector", lambda e: e.tensor_scalar(RS2, RS2, QB, None, ALU.mult))
            RBETA = RVAR
            P.run("vector", lambda e: e.tensor_tensor(RBETA, RALPHA, RS2, ALU.mult))
            RGAM = RSTD
            P.run("vector", lambda e: e.tensor_scalar(
                RGAM, RMAXU, SWO2, 0.5 / QB, ALU.mult, ALU.mult), inc=vV)
            BETB = PSIB
            GAMB = ALPB
            bcast_row(RBETA, BETB, 4)
            bcast_row(RGAM, GAMB, 5)
            y2sq = [arB[:, 1024:1024 + TB], arB[:, 1536:1536 + TB]]
            y2st = arB[:, 2048:2048 + TB]
            o2_cons = [0, 0]
            y2mm = {}
            o2sl4 = [arG_bf[:, 0:8192], arG_bf[:, 8192:16384]]
            rintv = {}
            P.wait("vector", sS)
            if True:
                for k in range(HC):
                    P.run("vector", lambda e, k=k: e.tensor_tensor(
                        TMP[0][:, 0:TB], wslot[k], BETB[:, 0:TB], ALU.mult))
                    P.run("vector", lambda e: e.tensor_scalar(
                        TMP[1][:, 0:TB], TMP[0][:, 0:TB], M32, M32, ALU.add, ALU.subtract))
                    P.run("vector", lambda e: e.tensor_scalar(
                        TMP[0][:, 0:TB], TMP[0][:, 0:TB], M64, M64, ALU.add, ALU.subtract))
                    P.run("vector", lambda e: e.tensor_tensor(
                        TMP[0][:, 0:TB], TMP[0][:, 0:TB], TMP[1][:, 0:TB], ALU.subtract))
                    P.run("vector", lambda e, k=k: e.scalar_tensor_tensor(
                        wslot[k], TMP[0][:, 0:TB], FLAG[:, 0:1], TMP[1][:, 0:TB],
                        ALU.mult, ALU.add), inc=vV)
                    rintv[k] = vV.n
            for oh in range(2):
                for kq in range(HC // 4):
                    bq_ = kq % 2
                    if o2_cons[bq_]:
                        P.wait("sync", tT, o2_cons[bq_])
                    elif oh == 0 and kq < 2:
                        P.wait("sync", tT)
                    to2q = bass.AP(TO2.tensor, TO2.offset + 4 * kq * 128 * 2048,
                                   [[2048, 128], [128 * 2048, 4], [1, 2048]])
                    P.run("sync", lambda e, bq_=bq_, to2q=to2q: e.dma_start(
                        out=o2sl4[bq_], in_=to2q), inc=dO2[bq_], inc_n=16)
                    for kk in range(4):
                        k = 4 * kq + kk
                        P.wait("tensor", dO2[bq_])
                        P.wait("tensor", vV, rintv[k])
                        if k == 0:
                            for o8 in range(8):
                                bank_wait(o8)
                        for o8 in range(8):
                            oc = 8 * oh + o8
                            P.run("tensor", lambda e, bq_=bq_, kk=kk, oc=oc, o8=o8, k=k, kq=kq: e.matmul(
                                ps[o8][:, 0:TB],
                                o2sl4[bq_][:, 2048 * kk + 128 * oc:2048 * kk + 128 * (oc + 1)],
                                wslot[k],
                                start=(kk == 0 and kq == 0), stop=(kq == HC // 4 - 1 and kk == 3)),
                                inc=(tT if o8 == 7 else None))
                    o2_cons[bq_] = tT.n
                for o8 in range(8):
                    oc = 8 * oh + o8
                    P.wait("vector", tT)
                    P.run("vector", lambda e, o8=o8: e.tensor_tensor(
                        y2st, ps[o8][:, 0:TB], GAMB[:, 0:TB], ALU.mult))
                    P.run("vector", lambda e, oc=oc: e.scalar_tensor_tensor(
                        X1[:, TB * oc:TB * (oc + 1)], X1[:, TB * oc:TB * (oc + 1)], 2.0,
                        y2st, ALU.mult, ALU.add), inc=vV)
                    bank_read(o8, vV)
            y2v = vV.n
            for c in range(OC):
                P.wait("scalar", vV, y2v)
                if c >= 2:
                    P.wait("scalar", tT, y2mm[c - 2])
                P.run("scalar", lambda e, c=c: e.activation(
                    y2sq[c % 2], X1[:, TB * c:TB * (c + 1)], AF.Square, bias=0.0, scale=1.0),
                    inc=sS)
                P.wait("tensor", sS)
                if c == 0:
                    bank_wait(2)
                    bank_wait(3)
                P.run("tensor", lambda e, c=c: e.matmul(
                    ps[2][0:1, 0:TB], ONEC, X1[:, TB * c:TB * (c + 1)],
                    start=(c == 0), stop=(c == OC - 1)))
                P.run("tensor", lambda e, c=c: e.matmul(
                    ps[3][0:1, 0:TB], ONEC, y2sq[c % 2],
                    start=(c == 0), stop=(c == OC - 1)), inc=tT)
                y2mm[c] = tT.n
            P.wait("scalar", tT)
            P.run("scalar", lambda e: e.activation(RM, ps[2][0:1, 0:TB], AF.Identity,
                                                   bias=0.0, scale=1.0 / D))
            P.run("scalar", lambda e: e.activation(RMSQ, ps[3][0:1, 0:TB], AF.Identity,
                                                   bias=0.0, scale=1.0 / D), inc=sS)
            bank_read(2, sS)
            bank_read(3, sS)
            P.wait("vector", sS)
            P.run("vector", lambda e: e.tensor_tensor(RVAR, RM, RM, ALU.mult))
            P.run("vector", lambda e: e.tensor_tensor(RVAR, RMSQ, RVAR, ALU.subtract), inc=vV)
            P.wait("scalar", vV)
            P.run("scalar", lambda e: e.activation(RSTD, RVAR, AF.Sqrt, bias=EPS, scale=1.0),
                  inc=sS)
            P.wait("vector", sS)
            P.run("vector", lambda e: e.reciprocal(RSTDr, RSTD))
            P.run("vector", lambda e: e.tensor_tensor(RB1, RM, RSTDr, ALU.mult), inc=vV)
            bcast_row(RSTDr, F1B, 4)
            bcast_row(RB1, B1B, 5)
            ost = [arB[:, 0:TB], arB[:, 512:512 + TB]]
            ost_st = {}
            P.wait("vector", sS)
            for c in range(KC):
                b = c % 2
                if c >= 2:
                    P.wait("vector", dOS[b])
                P.run("vector", lambda e, c=c, b=b: e.tensor_tensor(
                    ost[b], X1[:, TB * c:TB * (c + 1)], F1B[:, 0:TB], ALU.mult))
                P.run("vector", lambda e, c=c, b=b: e.tensor_tensor(
                    ost[b], ost[b], B1B[:, 0:TB], ALU.subtract), inc=vV)
                P.wait("sync", vV)
                P.run("sync", lambda e, c=c, b=b, tb=tb: e.dma_start(
                    out=OUT[c][:, TB * tb:TB * (tb + 1)], in_=ost[b]), inc=dOS[b], inc_n=16)
                ost_st[c] = dOS[b].n

        if debug:
            P.wait("sync", vV)
            P.run("sync", lambda e: e.dma_start(out=DBGY, in_=X1[:, 0:8192]), inc=dDB, inc_n=16)
            P.run("sync", lambda e: e.dma_start(out=DBGB[:, 0:512], in_=arD[:, 2 * TB:3 * TB]),
                  inc=dDB, inc_n=16)
            P.run("sync", lambda e: e.dma_start(out=DBGB[:, 512:1024], in_=arD[:, 3 * TB:4 * TB]),
                  inc=dDB, inc_n=16)
            P.run("sync", lambda e: e.dma_start(out=DBG[:, 0:16], in_=Acol), inc=dDB, inc_n=16)
            P.run("sync", lambda e: e.dma_start(out=DBG[:, 16:19], in_=sml[:, 104:107],
                                                allow_slow_non_contiguous=True),
                  inc=dDB, inc_n=16)
        for sem in (dQS, dTS[0], dTS[1], dOS[0], dOS[1], dDB):
            P.wait("sync", sem)

        for eng in _Prog.ENGS:
            getattr(block, eng)(lambda e, q=tuple(P.q[eng]): [f(e) for f in q])
    return nc


_NC_CACHE = {}


def kernel(**inputs):
    per_core = build_host(inputs)
    if "nc" not in _NC_CACHE:
        _NC_CACHE["nc"] = build_nc()
    nc = _NC_CACHE["nc"]
    res = run_bass_kernel_spmd(nc, per_core, core_ids=list(range(NCORES)))
    out = np.empty((NCORES, S, D), np.float32)
    for b in range(NCORES):
        o = np.asarray(res.results[b]["OUT"])
        out[b] = o.reshape(D, S).T
    return out


# version 13
# speedup vs baseline: 1.0606x; 1.0598x over previous
import contextlib
import numpy as np
import ml_dtypes
import concourse.bass as bass
import concourse.mybir as mybir
from concourse.bass_utils import run_bass_kernel_spmd

F32 = mybir.dt.float32
BF16 = mybir.dt.bfloat16
AF = mybir.ActivationFunctionType
ALU = mybir.AluOpType
AX = mybir.AxisListType

D = 2048
S = 2048
H = 8192
KC = 16
TB = 512
NTB = S // TB
ROT = D // 2
RC = 8
HC = 64
GC = 2 * HC
OC = 16
EPS = 1e-5
M32 = 12582912.0
M64 = 6755399441055744.0
QB = 127.0
ROPE_THETA = 10000.0
NCORES = 8


def _pi_perm():
    out = np.arange(D)
    for pos in range(ROT):
        c = pos // 128
        r = pos % 128
        out[pos] = 2 * (128 * (c // 2) + r) + (c % 2)
    return out


def _pair_tables():
    rot = np.float32(ROT)
    ar = np.arange(0, ROT, 2, dtype=np.float32)
    inv_freq = (np.float32(1.0) / np.float32(ROPE_THETA) ** (ar / rot)).astype(np.float32)
    tpos = np.arange(S, dtype=np.float32)
    freqs = (tpos[None, :] * inv_freq[:, None]).astype(np.float32)
    cosv = np.cos(freqs).astype(np.float32)
    sinv = np.sin(freqs).astype(np.float32)
    nb = RC // 2
    return (np.ascontiguousarray(cosv.reshape(nb, 128, S)),
            np.ascontiguousarray(sinv.reshape(nb, 128, S)))


def _img_tiles(mat_t):
    Dk, O = mat_t.shape
    nk, no = Dk // 128, O // 128
    a = mat_t.reshape(nk, 128, no, 128)
    return np.ascontiguousarray(a.transpose(2, 1, 0, 3).reshape(no, 128, nk * 128))


def build_host(inputs):
    X = inputs["L0_attn_X"]
    W_in = np.asarray(inputs["L0_attn_W_in"], np.float32)
    b_in = np.asarray(inputs["L0_attn_b_in"], np.float32)
    W_out = np.asarray(inputs["L0_attn_W_out"], np.float32)
    b_out = np.asarray(inputs["L0_attn_b_out"], np.float32)
    g = np.asarray(inputs["L0_ffn_norm_w"], np.float32)
    w_gv = np.asarray(inputs["L0_ffn_w_gv"], np.float32)
    w_o2 = np.asarray(inputs["L0_ffn_w_out"], np.float32)
    pi = _pi_perm()

    shared = {}
    Wqkv_t = np.concatenate([W_in[0:D][pi], W_in[D:2 * D][pi], W_in[2 * D:][pi]], axis=0).T
    shared["WQKV"] = _img_tiles(np.ascontiguousarray(Wqkv_t, dtype=np.float32))
    bqkv = np.concatenate([b_in[0:D][pi], b_in[D:2 * D][pi], b_in[2 * D:][pi]])
    shared["BIN"] = np.ascontiguousarray(bqkv.reshape(3 * KC, 128).T)
    shared["WOUT"] = _img_tiles(np.ascontiguousarray(W_out[:, pi].T, dtype=np.float32))
    shared["BOUT"] = np.ascontiguousarray(b_out.reshape(OC, 128).T)
    shared["G"] = np.ascontiguousarray(g.reshape(KC, 128).T)
    rho = np.empty(2 * H, dtype=np.int64)
    for c in range(HC):
        rho[256 * c:256 * c + 128] = np.arange(128 * c, 128 * (c + 1))
        rho[256 * c + 128:256 * c + 256] = H + np.arange(128 * c, 128 * (c + 1))
    shared["WGV"] = _img_tiles(np.ascontiguousarray(w_gv[rho].T, dtype=np.float32))
    shared["WO2"] = _img_tiles(np.ascontiguousarray(w_o2.T, dtype=np.float32))
    cos2, sin2 = _pair_tables()
    shared["COS"] = cos2
    shared["SIN"] = sin2
    shared["CONSTS"] = np.ascontiguousarray(
        np.broadcast_to(np.asarray([1.0, EPS, -0.5, 0.5], np.float32), (128, 4)))
    shared["ONESCOL"] = np.ones((128, 1), np.float32)
    shared["ONESROW"] = np.ones((1, 128), np.float32)
    shared["IDENT"] = np.eye(128, dtype=np.float32)

    per_core = []
    for b in range(NCORES):
        xb = np.asarray(X[b], np.float32).reshape(NTB, TB, KC, 128)
        img = np.ascontiguousarray(xb.transpose(0, 3, 2, 1).reshape(NTB, 128, KC * TB))
        m = {"XT": img}
        m.update(shared)
        per_core.append(m)
    return per_core


class _Sem:
    __slots__ = ("h", "n")

    def __init__(self, h):
        self.h = h
        self.n = 0


class _Prog:
    ENGS = ("tensor", "vector", "scalar", "sync", "gpsimd")

    def __init__(self):
        self.q = {e: [] for e in self.ENGS}

    def emit(self, eng, fn):
        self.q[eng].append(fn)

    def wait(self, eng, sem, n=None):
        if n is None:
            n = sem.n
        assert 0 <= n <= sem.n, f"wait {n} > scheduled {sem.n}"
        if n > 0:
            self.emit(eng, lambda e, h=sem.h, n=n: e.wait_ge(h, n))

    def run(self, eng, fn, inc=None, inc_n=1):
        if inc is not None:
            inc.n += inc_n

            def go(e, fn=fn, h=inc.h, k=inc_n):
                fn(e).then_inc(h, k)
        else:
            def go(e, fn=fn):
                fn(e)
        self.emit(eng, go)


def _rep0(base_ap, off, reps, width):
    pstep = base_ap.ap[0][0]
    return bass.AP(base_ap.tensor, base_ap.offset + off, [[pstep, 128], [0, reps], [1, width]])


def build_nc(debug=False):
    nc = bass.Bass("TRN2", target_bir_lowering=False, debug=False,
                   detect_race_conditions=False)
    XT = nc.dram_tensor("XT", [NTB, 128, KC * TB], F32, kind="ExternalInput").ap()
    WQKV = nc.dram_tensor("WQKV", [3 * KC, 128, KC * 128], F32, kind="ExternalInput").ap()
    BIN = nc.dram_tensor("BIN", [128, 3 * KC], F32, kind="ExternalInput").ap()
    WOUT = nc.dram_tensor("WOUT", [OC, 128, KC * 128], F32, kind="ExternalInput").ap()
    BOUT = nc.dram_tensor("BOUT", [128, OC], F32, kind="ExternalInput").ap()
    G = nc.dram_tensor("G", [128, KC], F32, kind="ExternalInput").ap()
    WGV = nc.dram_tensor("WGV", [GC, 128, KC * 128], F32, kind="ExternalInput").ap()
    WO2 = nc.dram_tensor("WO2", [OC, 128, HC * 128], F32, kind="ExternalInput").ap()
    COS = nc.dram_tensor("COS", [RC // 2, 128, S], F32, kind="ExternalInput").ap()
    SIN = nc.dram_tensor("SIN", [RC // 2, 128, S], F32, kind="ExternalInput").ap()
    CONSTS = nc.dram_tensor("CONSTS", [128, 4], F32, kind="ExternalInput").ap()
    ONESCOL = nc.dram_tensor("ONESCOL", [128, 1], F32, kind="ExternalInput").ap()
    ONESROW = nc.dram_tensor("ONESROW", [1, 128], F32, kind="ExternalInput").ap()
    IDENT = nc.dram_tensor("IDENT", [128, 128], F32, kind="ExternalInput").ap()
    OUT = nc.dram_tensor("OUT", [KC, 128, S], F32, kind="ExternalOutput").ap()
    ikind = "ExternalOutput" if debug else "Internal"
    TGV = nc.dram_tensor("TGV", [GC, 128, KC * 128], BF16, kind=ikind).ap()
    TO2 = nc.dram_tensor("TO2", [HC, 128, OC * 128], BF16, kind=ikind).ap()
    QBF = nc.dram_tensor("QBF", [NTB, 128, KC * TB], BF16, kind=ikind).ap()
    DBG = nc.dram_tensor("DBG", [128, 64], F32, kind="ExternalOutput").ap() if debug else None
    DBGY = nc.dram_tensor("DBGY", [128, 8192], F32, kind="ExternalOutput").ap() if debug else None
    DBGB = nc.dram_tensor("DBGB", [128, 1024], F32, kind="ExternalOutput").ap() if debug else None

    P = _Prog()
    es = contextlib.ExitStack()
    with es:
        arA = es.enter_context(nc.sbuf_tensor("arA", [128, 8192], F32))
        arB = es.enter_context(nc.sbuf_tensor("arB", [128, 6144], F32))
        arC = es.enter_context(nc.sbuf_tensor("arC", [128, 8192], F32))
        arD = es.enter_context(nc.sbuf_tensor("arD", [128, 6144], F32))
        arF = es.enter_context(nc.sbuf_tensor("arF", [128, 8192], F32))
        arG = es.enter_context(nc.sbuf_tensor("arG", [128, 8192], F32))
        wsl = es.enter_context(nc.sbuf_tensor("wsl", [128, 2048], F32))
        sml = es.enter_context(nc.sbuf_tensor("sml", [128, 176], F32))
        row = es.enter_context(nc.sbuf_tensor("rowbuf", [1, 3840], F32))
        idb = es.enter_context(nc.sbuf_tensor("idb", [128, 128], F32))
        ps = [es.enter_context(nc.psum_tensor(f"pb{i}", [128, 512], F32)) for i in range(8)]

        mksem = lambda name: _Sem(es.enter_context(nc.semaphore(name)))
        dPL = mksem("dPL")
        dSL = [mksem("dSL0"), mksem("dSL1")]
        dSLg = [mksem("dSLg0"), mksem("dSLg1")]
        dO2 = [mksem("dO20"), mksem("dO21")]
        dXT = mksem("dXT")
        dCOS = mksem("dCOS")
        dSIN = mksem("dSIN")
        dXB = mksem("dXB")
        dQB = mksem("dQB")
        dQS = mksem("dQS")
        dTS = [mksem("dTS0"), mksem("dTS1")]
        dOS = [mksem("dOS0"), mksem("dOS1")]
        dDB = mksem("dDB")
        gG = mksem("gG")
        tT = mksem("tT")
        vV = mksem("vV")
        sS = mksem("sS")
        block = es.enter_context(nc.Block())

        CT = sml[:, 0:4]
        ONEC = sml[:, 4:5]
        ONECB = sml[:, 5:6].bitcast(BF16)[:, 0:1]
        BINc = sml[:, 8:56]
        BOUTc = sml[:, 56:72]
        Gc = sml[:, 72:88]
        Acol = sml[:, 88:104]
        FLAG = sml[:, 104:105]
        RSGV = sml[:, 105:106]
        RSO2 = sml[:, 106:107]
        ONER = row[0:1, 0:128]
        SWGV = row[0:1, 132:133]
        SWO2 = row[0:1, 133:134]
        for v, c in ((1.0, 0), (EPS, 1), (-0.5, 2), (0.5, 3)):
            nc.const_aps.register(F32, v, CT[:, c:c + 1])

        for dst, src in ((sml[:, 0:4], CONSTS), (sml[:, 4:5], ONESCOL),
                         (row[0:1, 0:128], ONESROW), (idb[:], IDENT),
                         (BINc, BIN), (BOUTc, BOUT), (Gc, G)):
            P.run("sync", lambda e, dst=dst, src=src: e.dma_start(out=dst, in_=src),
                  inc=dPL, inc_n=16)
        P.wait("vector", dPL)
        P.run("vector", lambda e: e.tensor_scalar(FLAG, ONEC, M64, M64, ALU.add, ALU.subtract))
        P.run("vector", lambda e: e.tensor_copy(ONECB, ONEC))
        P.run("vector", lambda e: e.memset(Acol, 0.0), inc=vV)

        bank_tok = [None] * 8

        def bank_wait(bank):
            t = bank_tok[bank]
            if t is not None:
                P.wait("tensor", t[0], t[1])

        def bank_read(bank, sem):
            bank_tok[bank] = (sem, sem.n)

        qp_in = [arF[:, 0:4096], arF[:, 4096:8192]]
        arG_bft = arG[:].bitcast(BF16)
        qp_absl = [arG_bft[:, 0:4096], arG_bft[:, 4096:8192]]
        QCOLS = arD[:, 1024:1120]
        QS2 = arD[:, 1120:1122]

        def _quad(img, i, nsl=4):
            w = img.shape[2]
            return bass.AP(img.tensor, img.offset + i * nsl * 128 * w,
                           [[w, 128], [128 * w, nsl], [1, w]])

        QP_N = 96

        def qp_unit_src(i):
            if i < 64:
                return _quad(WGV, i, 2)
            j = i - 64
            return WO2[j // 2][:, 4096 * (j % 2):4096 * (j % 2 + 1)]

        qp_st = {"i": 0, "abs": {}, "red": {}, "fin": False, "fin_v": 0}

        def qp_step():
            st = qp_st
            i = st["i"]
            if i < QP_N:
                b = i % 2
                if i >= 2:
                    P.wait("sync", sS, st["abs"][i - 2])
                P.run("sync", lambda e, i=i, b=b: e.dma_start(out=qp_in[b],
                                                              in_=qp_unit_src(i)),
                      inc=dSL[b], inc_n=16)
            j = i - 1
            if 0 <= j < QP_N:
                b = j % 2
                P.wait("scalar", dSL[b])
                if j >= 2:
                    P.wait("scalar", vV, st["red"][j - 2])
                P.run("scalar", lambda e, b=b: e.activation(qp_absl[b], qp_in[b], AF.Abs,
                                                            bias=0.0, scale=1.0), inc=sS)
                st["abs"][j] = sS.n
            j = i - 2
            if 0 <= j < QP_N:
                b = j % 2
                P.wait("vector", sS, st["abs"][j])
                P.run("vector", lambda e, b=b, j=j: e.reduce_sum(QCOLS[:, j:j + 1],
                                                                 qp_absl[b], axis=AX.X),
                      inc=vV)
                st["red"][j] = vV.n
            st["i"] += 1

        def qp_fin():
            P.wait("vector", vV, qp_st["red"][QP_N - 1])
            P.run("vector", lambda e: e.reduce_sum(QS2[:, 0:1], QCOLS[:, 0:64], axis=AX.X),
                  inc=vV)
            P.wait("vector", vV)
            P.run("vector", lambda e: e.reduce_sum(QS2[:, 1:2], QCOLS[:, 64:96], axis=AX.X),
                  inc=vV)
            for (col, recip_col, srow_cell, count, rcell) in (
                    (0, RSGV, SWGV, GC * 128 * 2048, 140),
                    (1, RSO2, SWO2, OC * 128 * 8192, 141)):
                P.wait("tensor", vV)
                P.wait("tensor", dPL)
                bank_wait(6)
                P.run("tensor", lambda e, col=col: e.matmul(
                    ps[6][0:1, 0:1], QS2[:, col:col + 1], ONEC, start=True, stop=True),
                    inc=tT)
                P.wait("vector", tT)
                r0 = row[0:1, 136 + col:137 + col]
                P.run("vector", lambda e, r0=r0: e.tensor_copy(r0, ps[6][0:1, 0:1]), inc=vV)
                bank_read(6, vV)
                P.wait("vector", vV)
                P.run("vector", lambda e, r0=r0, srow_cell=srow_cell, count=count:
                      e.tensor_scalar(srow_cell, r0, 1.0 / count, 1e-5,
                                      ALU.mult, ALU.max), inc=vV)
                P.wait("vector", vV)
                rc = row[0:1, rcell:rcell + 1]
                P.run("vector", lambda e, rc=rc, srow_cell=srow_cell:
                      e.reciprocal(rc, srow_cell), inc=vV)
                P.wait("tensor", vV)
                bank_wait(6)
                P.run("tensor", lambda e, rc=rc: e.matmul(ps[6][:, 0:1], ONER, rc,
                                                          start=True, stop=True), inc=tT)
                P.wait("vector", tT)
                P.run("vector", lambda e, recip_col=recip_col:
                      e.tensor_copy(recip_col, ps[6][:, 0:1]), inc=vV)
                bank_read(6, vV)
            qp_st["fin_v"] = vV.n

        tn_in = [arF[:, 0:4096], arF[:, 4096:8192]]
        arG_bft = arG[:].bitcast(BF16)
        arD_bft = arD[:].bitcast(BF16)
        tn_t1 = [arG_bft[:, 0:4096], arG_bft[:, 8192:12288]]
        tn_t2 = [arG_bft[:, 4096:8192], arG_bft[:, 12288:16384]]
        tn_out = [arD_bft[:, 8192:12288], arD_bft[:, 8192:12288]]


        gv_in2 = lambda i: _quad(WGV, i, 2)
        gv_out2 = lambda i: _quad(TGV, i, 2)
        o2_in2 = lambda i: WO2[i // 2][:, 4096 * (i % 2):4096 * (i % 2 + 1)]

        def o2_out2(i):
            oc, kh = i // 2, i % 2
            return bass.AP(TO2.tensor, TO2.offset + 32 * kh * (128 * OC * 128) + 128 * oc,
                           [[OC * 128, 128], [128 * OC * 128, 32], [1, 128]])
        TN_UNITS = [(gv_in2, gv_out2, GC // 2, RSGV), (o2_in2, o2_out2, 2 * OC, RSO2)]
        tn_state = {"pass": 0, "i": 0, "sgn": {}, "add": {}}

        def tern_step():
            st = tn_state
            if st["pass"] >= len(TN_UNITS):
                return
            img, out_img, nunit, recip_col = TN_UNITS[st["pass"]]
            i, b = st["i"], st["i"] % 2
            key = (st["pass"], i - 2)
            use_v = False
            if key in st["sgn"]:
                P.wait("sync", sS, st["sgn"][key][0])
                P.wait("sync", vV, st["sgn"][key][1])
            elif st["pass"] == 0:
                P.wait("sync", sS, qp_st["abs"][QP_N - 2 + i])
            P.run("sync", lambda e, i=i, b=b, img=img: e.dma_start(out=tn_in[b], in_=img(i)),
                  inc=dSL[b], inc_n=16)
            if use_v:
                P.wait("vector", dSL[b])
                if key in st["add"]:
                    P.wait("vector", vV, st["add"][key])
                P.run("vector", lambda e, b=b, recip_col=recip_col: e.tensor_scalar(
                    tn_t1[b], tn_in[b], recip_col, 0.5, ALU.mult, ALU.is_ge))
                P.run("vector", lambda e, b=b, recip_col=recip_col: e.tensor_scalar(
                    tn_t2[b], tn_in[b], recip_col, -0.5, ALU.mult, ALU.is_le), inc=vV)
                st["sgn"][(st["pass"], i)] = (sS.n, vV.n)
                P.wait("vector", dTS[0])
                P.run("vector", lambda e, b=b: e.tensor_tensor(
                    tn_t1[b], tn_t1[b], tn_t2[b], ALU.subtract))
                P.run("vector", lambda e, b=b: e.tensor_scalar(
                    tn_out[b], tn_t1[b], 2.0, None, ALU.mult), inc=vV)
            else:
                P.wait("scalar", dSL[b])
                if key in st["add"]:
                    P.wait("scalar", vV, st["add"][key])
                else:
                    P.wait("scalar", vV, qp_st["fin_v"])
                P.run("scalar", lambda e, b=b, recip_col=recip_col: e.activation(
                    tn_t1[b], tn_in[b], AF.Sign, bias=-0.5, scale=recip_col))
                P.run("scalar", lambda e, b=b, recip_col=recip_col: e.activation(
                    tn_t2[b], tn_in[b], AF.Sign, bias=0.5, scale=recip_col), inc=sS)
                st["sgn"][(st["pass"], i)] = (sS.n, vV.n)
                P.wait("vector", sS)
                P.wait("vector", dTS[0])
                P.run("vector", lambda e, b=b: e.tensor_tensor(tn_out[b], tn_t1[b], tn_t2[b],
                                                               ALU.add), inc=vV)
            st["add"][(st["pass"], i)] = vV.n
            P.wait("sync", vV)
            P.run("sync", lambda e, i=i, b=b, out_img=out_img:
                  e.dma_start(out=out_img(i), in_=tn_out[b]), inc=dTS[0], inc_n=16)
            st["i"] += 1
            if st["i"] >= nunit:
                st["pass"] += 1
                st["i"] = 0

        def istep(n=1):
            for _ in range(n):
                if qp_st["i"] < QP_N + 2:
                    qp_step()
                elif not qp_st["fin"]:
                    qp_fin()
                    qp_st["fin"] = True
                else:
                    tern_step()

        def tern_flush():
            while (qp_st["i"] < QP_N + 2 or not qp_st["fin"]
                   or tn_state["pass"] < len(TN_UNITS)):
                istep()

        P.wait("scalar", dPL)
        P.wait("vector", dPL)
        P.wait("tensor", dPL)
        P.wait("tensor", vV)

        arC_bf = arC[:].bitcast(BF16)
        arA_bf = arA[:].bitcast(BF16)
        wslb = wsl[:].bitcast(BF16)
        slab = [wslb[:, 0:2048], wslb[:, 2048:4096]]
        slab_cons = [0, 0]
        slab_sem = [dSL[0], dSL[1]]

        def load_slab(img, c, si, eng="gpsimd"):
            b = si % 2
            if slab_cons[b] > 0:
                P.wait(eng, tT, slab_cons[b])
            sem = dSLg[b] if eng == "gpsimd" else dSL[b]
            slab_sem[b] = sem
            P.run(eng, lambda e, img=img, c=c, b=b: e.dma_start(out=slab[b], in_=img[c]),
                  inc=sem, inc_n=16)
            return b

        def mm_chunk(b, rhs_bf, psbank, extra_wait=None):
            P.wait("tensor", slab_sem[b])
            bank_wait(psbank)
            if extra_wait is not None:
                P.wait("tensor", extra_wait)
            rhs_of = rhs_bf if callable(rhs_bf) else (
                lambda k, r=rhs_bf: r[:, TB * k:TB * (k + 1)])
            for k in range(KC):
                P.run("tensor", lambda e, b=b, k=k, psbank=psbank, rhs_of=rhs_of:
                      e.matmul(ps[psbank][:, 0:TB], slab[b][:, 128 * k:128 * (k + 1)],
                               rhs_of(k), start=(k == 0), stop=(k == KC - 1)),
                      inc=(tT if k == KC - 1 else None))

        XTC = arC_bf

        def xt_tile(tl, k):
            return XTC[:, 8192 * tl + 512 * k:8192 * tl + 512 * (k + 1)]

        def mm_group(b, tl, bank):
            P.wait("tensor", slab_sem[b])
            bank_wait(bank)
            for k in range(KC):
                P.run("tensor", lambda e, b=b, tl=tl, k=k, bank=bank:
                      e.matmul(ps[bank][:, 0:TB], slab[b][:, 128 * k:128 * (k + 1)],
                               xt_tile(tl, k), start=(k == 0), stop=(k == KC - 1)),
                      inc=(tT if k == KC - 1 else None))

        arD_bf = arD[:].bitcast(BF16)
        COSb = arD_bf[:, 0:1024]
        SINb = arD_bf[:, 1024:2048]
        rBt = [arD[:, 2048 + 512 * t:2048 + 512 * (t + 1)] for t in range(NTB)]
        si = [0]
        HW_ = 1024

        KB0 = arB[:, 0:HW_]
        KB1 = arB[:, HW_:2 * HW_]
        VST = arA[:, 0:HW_]
        TM1 = arA[:, 2048:2048 + HW_]
        TM2 = arA[:, 4096:4096 + HW_]
        PBUF = arA[:, 6144:6144 + HW_]
        SRED2 = sml[:, 108:110]
        SRED1 = sml[:, 112:113]
        QEb = arA_bf[:, 0:HW_]
        QOb = arA_bf[:, HW_:2 * HW_]
        QPb = arA_bf[:, 2 * HW_:3 * HW_]
        sqb = [arA[:, 512 * t:512 * (t + 1)] for t in range(2)]

        for hf in range(2):
            gtb = (2 * hf, 2 * hf + 1)
            if hf > 0:
                P.wait("gpsimd", tT)
            for tl in range(2):
                P.run("gpsimd", lambda e, hf=hf, tl=tl: e.dma_start(
                    out=XTC[:, 8192 * tl:8192 * (tl + 1)], in_=XT[2 * hf + tl]),
                    inc=dXT, inc_n=16)

            statmm = {}
            P.wait("tensor", dXT)
            for c in range(KC):
                b = load_slab(WQKV, KC + c, si[0]); si[0] += 1
                for tl in range(2):
                    mm_group(b, tl, tl)
                    if tl == 1:
                        slab_cons[b] = tT.n
                    P.wait("scalar", tT)
                    if (c - 1, tl) in statmm:
                        P.wait("scalar", tT, statmm[(c - 1, tl)])
                    P.run("scalar", lambda e, tl=tl, c=c: e.activation(
                        sqb[tl], ps[tl][:, 0:TB], AF.Square,
                        bias=BINc[:, KC + c:KC + c + 1], scale=1.0), inc=sS)
                    bank_read(tl, sS)
                    P.wait("tensor", sS)
                    if c == 0:
                        bank_wait(4 + tl)
                    P.run("tensor", lambda e, c=c, tl=tl: e.matmul(
                        ps[4 + tl][0:1, 0:TB], ONEC, sqb[tl],
                        start=(c == 0), stop=(c == KC - 1)), inc=tT)
                    statmm[(c, tl)] = tT.n
                istep()
            P.wait("scalar", tT)
            for tl in range(2):
                RT0 = row[0:1, 256 + 512 * tl:256 + 512 * (tl + 1)]
                P.run("scalar", lambda e, RT0=RT0, tl=tl: e.activation(
                    RT0, ps[4 + tl][0:1, 0:TB], AF.Sqrt, bias=0.0, scale=1.0), inc=sS)
                bank_read(4 + tl, sS)
            P.wait("vector", sS)
            for tl in range(2):
                RT0 = row[0:1, 256 + 512 * tl:256 + 512 * (tl + 1)]
                P.run("vector", lambda e, RT0=RT0: e.tensor_scalar(RT0, RT0, 1e-12, None,
                                                                   ALU.max))
                P.run("vector", lambda e, RT0=RT0: e.reciprocal(RT0, RT0),
                      inc=(vV if tl == 1 else None))
            P.wait("tensor", vV)
            for tl in range(2):
                RT0 = row[0:1, 256 + 512 * tl:256 + 512 * (tl + 1)]
                bank_wait(tl)
                P.run("tensor", lambda e, RT0=RT0, tl=tl: e.matmul(
                    ps[tl][:, 0:TB], ONER, RT0, start=True, stop=True), inc=tT)
            P.wait("scalar", tT)
            for tl in range(2):
                P.run("scalar", lambda e, tl=tl, hf=hf: e.copy(rBt[2 * hf + tl],
                                                               ps[tl][:, 0:TB]), inc=sS)
                bank_read(tl, sS)

            kb_read = [0, 0]
            rope_done = 0

            def proj_chunk(wcol, dstbuf, bias_col, read_guard=None):
                b = load_slab(WQKV, wcol, si[0]); si[0] += 1
                for tl in range(2):
                    mm_group(b, tl, tl)
                    if tl == 1:
                        slab_cons[b] = tT.n
                    P.wait("vector", tT)
                    if tl == 0 and read_guard:
                        P.wait("vector", vV, read_guard)
                    P.run("vector", lambda e, tl=tl, dstbuf=dstbuf, bias_col=bias_col:
                          e.tensor_scalar(dstbuf[:, 512 * tl:512 * (tl + 1)],
                                          ps[tl][:, 0:TB], bias_col, None, ALU.add),
                          inc=vV)
                    bank_read(tl, vV)
                istep(2)

            def rope_pair(E, O, outE=None, outO=None):
                P.run("vector", lambda e, O=O: e.tensor_tensor(TM1, O, SINb, ALU.mult))
                P.run("vector", lambda e, E=E: e.tensor_tensor(TM2, E, SINb, ALU.mult))
                P.run("vector", lambda e, E=E: e.tensor_tensor(E, E, COSb, ALU.mult))
                P.run("vector", lambda e, E=E, outE=outE: e.tensor_tensor(
                    outE if outE is not None else E, E, TM1, ALU.subtract))
                P.run("vector", lambda e, O=O: e.tensor_tensor(O, O, COSb, ALU.mult))
                P.run("vector", lambda e, O=O, outO=outO: e.tensor_tensor(
                    outO if outO is not None else O, O, TM2, ALU.add), inc=vV)

            def a_accum(krbuf, cv):
                P.run("vector", lambda e, krbuf=krbuf: e.tensor_tensor(PBUF, krbuf, VST,
                                                                       ALU.mult))
                for tl in range(2):
                    P.run("vector", lambda e, tl=tl, hf=hf: e.tensor_tensor(
                        PBUF[:, 512 * tl:512 * (tl + 1)], PBUF[:, 512 * tl:512 * (tl + 1)],
                        rBt[2 * hf + tl], ALU.mult))
                p3 = bass.AP(arA[:].tensor, arA[:].offset + 6144,
                             [[8192, 128], [512, 2], [1, 512]])
                P.run("vector", lambda e, p3=p3: e.reduce_sum(SRED2, p3, axis=AX.X), inc=vV)
                P.wait("vector", vV)
                P.run("vector", lambda e: e.reduce_sum(SRED1, SRED2, axis=AX.X), inc=vV)
                P.wait("vector", vV)
                P.run("vector", lambda e, cv=cv: e.tensor_tensor(
                    Acol[:, cv:cv + 1], Acol[:, cv:cv + 1], SRED1, ALU.add), inc=vV)

            for pb in range(RC // 2):
                c0, c1 = 2 * pb, 2 * pb + 1
                P.wait("gpsimd", vV, rope_done)
                P.run("gpsimd", lambda e, pb=pb, hf=hf: e.dma_start(
                    out=COSb, in_=COS[pb][:, HW_ * hf:HW_ * (hf + 1)]), inc=dCOS, inc_n=16)
                P.run("gpsimd", lambda e, pb=pb, hf=hf: e.dma_start(
                    out=SINb, in_=SIN[pb][:, HW_ * hf:HW_ * (hf + 1)]), inc=dSIN, inc_n=16)
                proj_chunk(KC + c0, KB0, BINc[:, KC + c0:KC + c0 + 1], read_guard=kb_read[0])
                proj_chunk(KC + c1, KB1, BINc[:, KC + c1:KC + c1 + 1], read_guard=kb_read[1])
                P.wait("vector", dCOS)
                P.wait("vector", dSIN)
                rope_pair(KB0, KB1)
                rope_done = vV.n
                proj_chunk(2 * KC + c0, VST, BINc[:, 2 * KC + c0:2 * KC + c0 + 1],
                           read_guard=vV.n)
                a_accum(KB0, c0)
                proj_chunk(2 * KC + c1, VST, BINc[:, 2 * KC + c1:2 * KC + c1 + 1],
                           read_guard=vV.n)
                a_accum(KB1, c1)
                kb_read = [vV.n, vV.n]
            for c in range(RC, KC):
                proj_chunk(KC + c, KB0, BINc[:, KC + c:KC + c + 1], read_guard=kb_read[0])
                proj_chunk(2 * KC + c, VST, BINc[:, 2 * KC + c:2 * KC + c + 1],
                           read_guard=vV.n)
                a_accum(KB0, c)
                kb_read[0] = vV.n

            def q_spill_chunk(src_bf, c):
                for tl in range(2):
                    P.run("sync", lambda e, tl=tl, c=c, hf=hf, src_bf=src_bf: e.dma_start(
                        out=QBF[2 * hf + tl][:, 512 * c:512 * (c + 1)],
                        in_=src_bf[:, 512 * tl:512 * (tl + 1)]), inc=dQS, inc_n=16)

            q_guard = kb_read[0]
            for pb in range(RC // 2):
                c0, c1 = 2 * pb, 2 * pb + 1
                P.wait("gpsimd", vV, rope_done)
                P.run("gpsimd", lambda e, pb=pb, hf=hf: e.dma_start(
                    out=COSb, in_=COS[pb][:, HW_ * hf:HW_ * (hf + 1)]), inc=dCOS, inc_n=16)
                P.run("gpsimd", lambda e, pb=pb, hf=hf: e.dma_start(
                    out=SINb, in_=SIN[pb][:, HW_ * hf:HW_ * (hf + 1)]), inc=dSIN, inc_n=16)
                proj_chunk(c0, KB0, BINc[:, c0:c0 + 1], read_guard=q_guard)
                proj_chunk(c1, KB1, BINc[:, c1:c1 + 1])
                P.wait("vector", dCOS)
                P.wait("vector", dSIN)
                P.wait("vector", dQS)
                rope_pair(KB0, KB1, outE=QEb, outO=QOb)
                rope_done = vV.n
                q_guard = vV.n
                P.wait("sync", vV)
                q_spill_chunk(QEb, c0)
                q_spill_chunk(QOb, c1)
            for c in range(RC, KC):
                b = load_slab(WQKV, c, si[0]); si[0] += 1
                for tl in range(2):
                    mm_group(b, tl, tl)
                    if tl == 1:
                        slab_cons[b] = tT.n
                    P.wait("vector", tT)
                    if tl == 0:
                        P.wait("vector", dQS)
                    P.run("vector", lambda e, tl=tl, c=c: e.tensor_scalar(
                        QPb[:, 512 * tl:512 * (tl + 1)], ps[tl][:, 0:TB],
                        BINc[:, c:c + 1], None, ALU.add), inc=vV)
                    bank_read(tl, vV)
                istep(2)
                P.wait("sync", vV)
                q_spill_chunk(QPb, c)

        tern_flush()
        ph1_t, ph1_v, ph1_s = tT.n, vV.n, sS.n

        arG_bf = arG[:].bitcast(BF16)
        Qbf = arG_bf[:, 0:KC * TB]
        yslot = [arG_bf[:, KC * TB:KC * TB + TB]] + \
                [arG_bf[:, TB * c:TB * (c + 1)] for c in range(KC - 1)]
        Xbf2 = arC_bf[:, 0:KC * TB]
        YB = arA
        X1 = arC
        arD_bf2 = arD[:].bitcast(BF16)
        Q1 = arD_bf2[:, 4096:4096 + KC * TB]
        arF_bf2 = arF[:].bitcast(BF16)
        wslot = [arA[:].bitcast(BF16)[:, TB * i:TB * (i + 1)] for i in range(32)] + \
                [arF_bf2[:, TB * i:TB * (i + 1)] for i in range(32)]
        o2sl = [arG_bf[:, 0:HC * 128], arG_bf[:, HC * 128:2 * HC * 128]]
        TMP = [arB[:, 0:512], arB[:, 512:1024]]
        WMXr = arB[:, 3072:3072 + TB]
        WMNr = arB[:, 3584:3584 + TB]

        RM = row[0:1, 256:256 + TB]
        RMSQ = row[0:1, 768:768 + TB]
        RVAR = row[0:1, 1280:1280 + TB]
        RSTD = row[0:1, 1792:1792 + TB]
        RB1 = row[0:1, 2304:2304 + TB]
        RSTDr = row[0:1, 2816:2816 + TB]
        MROW = row[0:1, 3328:3328 + TB]
        F1B = arD[:, 0:TB]
        B1B = arD[:, TB:2 * TB]
        PSIB = arD[:, 2 * TB:3 * TB]
        ALPB = arD[:, 3 * TB:4 * TB]

        def bcast_row(src_row, dst, bank):
            P.wait("tensor", vV)
            P.wait("tensor", sS)
            bank_wait(bank)
            P.run("tensor", lambda e, src_row=src_row, bank=bank: e.matmul(
                ps[bank][:, 0:TB], ONER, src_row, start=True, stop=True), inc=tT)
            P.wait("scalar", tT)
            P.run("scalar", lambda e, dst=dst, bank=bank: e.copy(dst, ps[bank][:, 0:TB]),
                  inc=sS)
            bank_read(bank, sS)

        def colmax_to_row(src_tile, dst_row):
            P.wait("tensor", vV)
            bank_wait(5)
            for j in range(TB // 128):
                P.run("tensor", lambda e, j=j: e.transpose(
                    ps[5][:, 128 * j:128 * (j + 1)], src_tile[:, 128 * j:128 * (j + 1)],
                    idb[:]), inc=(tT if j == TB // 128 - 1 else None))
            P.wait("vector", tT)
            for j in range(TB // 128):
                P.run("vector", lambda e, j=j: e.reduce_max(
                    arB[:, 4096 + 2 * j:4096 + 2 * j + 1], ps[5][:, 128 * j:128 * (j + 1)],
                    axis=AX.X), inc=(vV if j == TB // 128 - 1 else None))
            bank_read(5, vV)
            P.wait("tensor", vV)
            bank_wait(6)
            for j in range(TB // 128):
                P.run("tensor", lambda e, j=j: e.transpose(
                    ps[6][0:1, 128 * j:128 * (j + 1)], arB[:, 4096 + 2 * j:4096 + 2 * j + 1],
                    idb[:]), inc=(tT if j == TB // 128 - 1 else None))
            P.wait("scalar", tT)
            P.run("scalar", lambda e, dst_row=dst_row: e.copy(dst_row, ps[6][0:1, 0:TB]),
                  inc=sS)
            bank_read(6, sS)

        for tb in range(NTB):
            if tb == 0:
                P.wait("sync", tT, ph1_t)
                P.wait("sync", vV, ph1_v)
                P.wait("sync", dQS)
                P.wait("sync", dTS[0])
                P.wait("sync", dTS[1])
                P.wait("gpsimd", tT, ph1_t)
                P.wait("gpsimd", vV, ph1_v)
            else:
                P.wait("sync", tT)
                P.wait("sync", vV)
                P.wait("gpsimd", vV)
            P.run("sync", lambda e, tb=tb: e.dma_start(out=Qbf, in_=QBF[tb]), inc=dQB, inc_n=16)
            P.run("gpsimd", lambda e, tb=tb: e.dma_start(out=Xbf2, in_=XT[tb]), inc=dXB, inc_n=16)

            P.wait("vector", dQB)
            for c in range(KC):
                P.run("vector", lambda e, c=c: e.tensor_scalar(
                    yslot[c], Qbf[:, TB * c:TB * (c + 1)],
                    Acol[:, c:c + 1], None, ALU.mult), inc=(vV if c == KC - 1 else None))

            for c in range(OC):
                b = load_slab(WOUT, c, si[0]); si[0] += 1
                mm_chunk(b, (lambda k: yslot[k]), c % 4, extra_wait=(vV if c == 0 else None))
                slab_cons[b] = tT.n
                P.wait("scalar", tT)
                if c == 0:
                    P.wait("scalar", vV)
                    P.wait("scalar", tT)
                P.run("scalar", lambda e, c=c: e.activation(
                    YB[:, TB * c:TB * (c + 1)], ps[c % 4][:, 0:TB], AF.Identity,
                    bias=BOUTc[:, c:c + 1], scale=1.0), inc=sS)
                bank_read(c % 4, sS)
            P.wait("vector", sS)
            P.wait("vector", dXB)
            P.run("vector", lambda e: e.scalar_tensor_tensor(
                YB[:, 0:KC * TB], Xbf2, 2.0, YB[:, 0:KC * TB], ALU.mult, ALU.add), inc=vV)
            sq = [arB[:, 1024:1024 + TB], arB[:, 1536:1536 + TB]]
            sqmm = {}
            P.wait("scalar", vV)
            for c in range(KC):
                bq = c % 2
                if c >= 2:
                    P.wait("scalar", tT, sqmm[c - 2])
                P.run("scalar", lambda e, c=c, bq=bq: e.activation(
                    sq[bq], YB[:, TB * c:TB * (c + 1)], AF.Square, bias=0.0, scale=1.0),
                    inc=sS)
                P.wait("tensor", sS)
                if c == 0:
                    bank_wait(2)
                    bank_wait(3)
                P.run("tensor", lambda e, c=c: e.matmul(
                    ps[2][0:1, 0:TB], ONEC, YB[:, TB * c:TB * (c + 1)],
                    start=(c == 0), stop=(c == KC - 1)))
                P.run("tensor", lambda e, c=c, bq=bq: e.matmul(
                    ps[3][0:1, 0:TB], ONEC, sq[bq],
                    start=(c == 0), stop=(c == KC - 1)), inc=tT)
                sqmm[c] = tT.n
            P.wait("scalar", tT)
            P.run("scalar", lambda e: e.activation(RM, ps[2][0:1, 0:TB], AF.Identity,
                                                   bias=0.0, scale=1.0 / D))
            P.run("scalar", lambda e: e.activation(RMSQ, ps[3][0:1, 0:TB], AF.Identity,
                                                   bias=0.0, scale=1.0 / D), inc=sS)
            bank_read(2, sS)
            bank_read(3, sS)
            P.wait("vector", sS)
            P.run("vector", lambda e: e.tensor_tensor(RVAR, RM, RM, ALU.mult))
            P.run("vector", lambda e: e.tensor_tensor(RVAR, RMSQ, RVAR, ALU.subtract), inc=vV)
            P.wait("scalar", vV)
            P.run("scalar", lambda e: e.activation(RSTD, RVAR, AF.Sqrt, bias=EPS, scale=1.0),
                  inc=sS)
            P.wait("vector", sS)
            P.run("vector", lambda e: e.reciprocal(RSTDr, RSTD))
            P.run("vector", lambda e: e.tensor_tensor(RB1, RM, RSTDr, ALU.mult), inc=vV)
            bcast_row(RSTDr, F1B, 4)
            bcast_row(RB1, B1B, 5)
            P.wait("vector", sS)
            f1rep = _rep0(arD[:], 0, KC, TB)
            b1rep = _rep0(arD[:], TB, KC, TB)
            P.run("vector", lambda e, f1rep=f1rep: e.tensor_tensor(
                X1[:, 0:KC * TB], YB[:, 0:KC * TB], f1rep, ALU.mult))
            P.run("vector", lambda e, b1rep=b1rep: e.tensor_tensor(
                X1[:, 0:KC * TB], X1[:, 0:KC * TB], b1rep, ALU.subtract), inc=vV)

            for c in range(KC):
                P.run("vector", lambda e, c=c: e.tensor_scalar(
                    YB[:, TB * c:TB * (c + 1)], X1[:, TB * c:TB * (c + 1)],
                    Gc[:, c:c + 1], None, ALU.mult), inc=(vV if c == KC - 1 else None))
            MX = arB[:, 2048:2048 + TB]
            MN = arB[:, 2560:2560 + TB]
            P.run("vector", lambda e: e.tensor_tensor(MX, YB[:, 0:TB], YB[:, TB:2 * TB], ALU.max))
            P.run("vector", lambda e: e.tensor_tensor(MN, YB[:, 0:TB], YB[:, TB:2 * TB], ALU.min))
            for c in range(2, KC):
                P.run("vector", lambda e, c=c: e.tensor_tensor(
                    MX, MX, YB[:, TB * c:TB * (c + 1)], ALU.max))
                P.run("vector", lambda e, c=c: e.tensor_tensor(
                    MN, MN, YB[:, TB * c:TB * (c + 1)], ALU.min))
            P.run("vector", lambda e: e.tensor_scalar(MN, MN, -1.0, None, ALU.mult))
            P.run("vector", lambda e: e.tensor_tensor(MX, MX, MN, ALU.max), inc=vV)
            colmax_to_row(MX, MROW)
            RD = RSTD
            RDr = RB1
            P.wait("vector", sS)
            P.run("vector", lambda e: e.tensor_scalar(RD, RVAR, EPS, None, ALU.add))
            P.run("vector", lambda e: e.reciprocal(RDr, RD))
            P.run("vector", lambda e: e.tensor_tensor(RD, RVAR, RDr, ALU.mult), inc=vV)
            P.wait("scalar", vV)
            P.run("scalar", lambda e: e.activation(RDr, RD, AF.Sqrt, bias=EPS, scale=1.0),
                  inc=sS)
            P.wait("vector", sS)
            P.run("vector", lambda e: e.reciprocal(RD, RDr))
            RMAXH = RMSQ
            P.run("vector", lambda e: e.tensor_tensor(RMAXH, RD, MROW, ALU.mult))
            P.run("vector", lambda e: e.tensor_scalar(RMAXH, RMAXH, 1e-5, None, ALU.max))
            RS1 = RM
            P.run("vector", lambda e: e.reciprocal(RS1, RMAXH))
            P.run("vector", lambda e: e.tensor_scalar(RS1, RS1, QB, None, ALU.mult))
            RPSI = RVAR
            P.run("vector", lambda e: e.tensor_tensor(RPSI, RD, RS1, ALU.mult))
            RALPHA = RSTDr
            P.run("vector", lambda e: e.tensor_scalar(
                RALPHA, RMAXH, SWGV, 0.5 / QB, ALU.mult, ALU.mult), inc=vV)
            bcast_row(RPSI, PSIB, 4)
            bcast_row(RALPHA, ALPB, 5)
            P.wait("vector", sS)
            psirep = _rep0(arD[:], 2 * TB, KC, TB)
            P.run("vector", lambda e, psirep=psirep: e.tensor_tensor(
                YB[:, 0:KC * TB], YB[:, 0:KC * TB], psirep, ALU.mult), inc=vV)
            for c in range(KC):
                zc = YB[:, TB * c:TB * (c + 1)]
                P.run("vector", lambda e, zc=zc: e.tensor_scalar(
                    TMP[0][:, 0:TB], zc, M32, M32, ALU.add, ALU.subtract))
                P.run("vector", lambda e, zc=zc: e.tensor_scalar(
                    TMP[1][:, 0:TB], zc, M64, M64, ALU.add, ALU.subtract))
                P.run("vector", lambda e: e.tensor_tensor(
                    TMP[1][:, 0:TB], TMP[1][:, 0:TB], TMP[0][:, 0:TB], ALU.subtract))
                P.run("vector", lambda e, c=c: e.scalar_tensor_tensor(
                    Q1[:, TB * c:TB * (c + 1)], TMP[1][:, 0:TB], FLAG[:, 0:1],
                    TMP[0][:, 0:TB], ALU.mult, ALU.add), inc=(vV if c == KC - 1 else None))

            WMX = WMXr
            WMN = WMNr
            P.run("vector", lambda e: e.memset(WMX, 0.0))
            P.run("vector", lambda e: e.memset(WMN, 0.0), inc=vV)
            gvg = [arB[:, 4608:4608 + TB], arB[:, 5120:5120 + TB]]
            gvs = [TMP[0][:, 0:TB], TMP[1][:, 0:TB]]
            for h in range(HC):
                par = h % 2
                vbank, gbank = (6, 7) if par == 0 else (4, 5)
                bv_ = load_slab(TGV, 2 * h, si[0], eng="sync"); si[0] += 1
                mm_chunk(bv_, Q1, vbank, extra_wait=(vV if h == 0 else None))
                slab_cons[bv_] = tT.n
                bg_ = load_slab(TGV, 2 * h + 1, si[0], eng="sync"); si[0] += 1
                mm_chunk(bg_, Q1, gbank)
                slab_cons[bg_] = tT.n
                P.wait("vector", tT)
                P.run("vector", lambda e, par=par, gbank=gbank: e.tensor_tensor(
                    gvg[par], ps[gbank][:, 0:TB], ALPB[:, 0:TB], ALU.mult), inc=vV)
                P.wait("scalar", vV)
                P.run("scalar", lambda e, par=par: e.activation(gvs[par], gvg[par],
                                                                AF.Sigmoid, bias=0.0, scale=1.0),
                      inc=sS)
                P.wait("vector", sS)
                P.run("vector", lambda e, par=par, vbank=vbank: e.tensor_tensor(
                    gvs[par], ps[vbank][:, 0:TB], gvs[par], ALU.mult))
                P.run("vector", lambda e, h=h, par=par: e.tensor_tensor(
                    wslot[h], gvs[par], gvg[par], ALU.mult))
                P.run("vector", lambda e, h=h: e.tensor_tensor(
                    WMX, WMX, wslot[h], ALU.max))
                P.run("vector", lambda e, h=h: e.tensor_tensor(
                    WMN, WMN, wslot[h], ALU.min), inc=vV)
                bank_read(vbank, vV)
                bank_read(gbank, vV)
            P.run("vector", lambda e: e.tensor_scalar(WMN, WMN, -1.0, None, ALU.mult))
            P.run("vector", lambda e: e.tensor_tensor(WMX, WMX, WMN, ALU.max), inc=vV)
            WROW = MROW
            colmax_to_row(WMX, WROW)
            RMAXU = RM
            P.wait("vector", sS)
            P.run("vector", lambda e: e.tensor_tensor(RMAXU, RALPHA, WROW, ALU.mult))
            P.run("vector", lambda e: e.tensor_scalar(RMAXU, RMAXU, 1e-5, None, ALU.max))
            RS2 = RMSQ
            P.run("vector", lambda e: e.reciprocal(RS2, RMAXU))
            P.run("vector", lambda e: e.tensor_scalar(RS2, RS2, QB, None, ALU.mult))
            RBETA = RVAR
            P.run("vector", lambda e: e.tensor_tensor(RBETA, RALPHA, RS2, ALU.mult))
            RGAM = RSTD
            P.run("vector", lambda e: e.tensor_scalar(
                RGAM, RMAXU, SWO2, 0.5 / QB, ALU.mult, ALU.mult), inc=vV)
            BETB = PSIB
            GAMB = ALPB
            bcast_row(RBETA, BETB, 4)
            bcast_row(RGAM, GAMB, 5)
            y2sq = [arB[:, 1024:1024 + TB], arB[:, 1536:1536 + TB]]
            y2st = arB[:, 2048:2048 + TB]
            o2_cons = [0, 0]
            y2mm = {}
            o2sl4 = [arG_bf[:, 0:8192], arG_bf[:, 8192:16384]]
            rintv = {}
            P.wait("vector", sS)
            if True:
                for k in range(HC):
                    P.run("vector", lambda e, k=k: e.tensor_tensor(
                        TMP[0][:, 0:TB], wslot[k], BETB[:, 0:TB], ALU.mult))
                    P.run("vector", lambda e: e.tensor_scalar(
                        TMP[1][:, 0:TB], TMP[0][:, 0:TB], M32, M32, ALU.add, ALU.subtract))
                    P.run("vector", lambda e: e.tensor_scalar(
                        TMP[0][:, 0:TB], TMP[0][:, 0:TB], M64, M64, ALU.add, ALU.subtract))
                    P.run("vector", lambda e: e.tensor_tensor(
                        TMP[0][:, 0:TB], TMP[0][:, 0:TB], TMP[1][:, 0:TB], ALU.subtract))
                    P.run("vector", lambda e, k=k: e.scalar_tensor_tensor(
                        wslot[k], TMP[0][:, 0:TB], FLAG[:, 0:1], TMP[1][:, 0:TB],
                        ALU.mult, ALU.add), inc=vV)
                    rintv[k] = vV.n
            for oh in range(2):
                for kq in range(HC // 4):
                    bq_ = kq % 2
                    if o2_cons[bq_]:
                        P.wait("sync", tT, o2_cons[bq_])
                    elif oh == 0 and kq < 2:
                        P.wait("sync", tT)
                    to2q = bass.AP(TO2.tensor, TO2.offset + 4 * kq * 128 * 2048,
                                   [[2048, 128], [128 * 2048, 4], [1, 2048]])
                    P.run("sync", lambda e, bq_=bq_, to2q=to2q: e.dma_start(
                        out=o2sl4[bq_], in_=to2q), inc=dO2[bq_], inc_n=16)
                    for kk in range(4):
                        k = 4 * kq + kk
                        P.wait("tensor", dO2[bq_])
                        P.wait("tensor", vV, rintv[k])
                        if k == 0:
                            for o8 in range(8):
                                bank_wait(o8)
                        for o8 in range(8):
                            oc = 8 * oh + o8
                            P.run("tensor", lambda e, bq_=bq_, kk=kk, oc=oc, o8=o8, k=k, kq=kq: e.matmul(
                                ps[o8][:, 0:TB],
                                o2sl4[bq_][:, 2048 * kk + 128 * oc:2048 * kk + 128 * (oc + 1)],
                                wslot[k],
                                start=(kk == 0 and kq == 0), stop=(kq == HC // 4 - 1 and kk == 3)),
                                inc=(tT if o8 == 7 else None))
                    o2_cons[bq_] = tT.n
                for o8 in range(8):
                    oc = 8 * oh + o8
                    P.wait("vector", tT)
                    P.run("vector", lambda e, o8=o8: e.tensor_tensor(
                        y2st, ps[o8][:, 0:TB], GAMB[:, 0:TB], ALU.mult))
                    P.run("vector", lambda e, oc=oc: e.scalar_tensor_tensor(
                        X1[:, TB * oc:TB * (oc + 1)], X1[:, TB * oc:TB * (oc + 1)], 2.0,
                        y2st, ALU.mult, ALU.add), inc=vV)
                    bank_read(o8, vV)
            y2v = vV.n
            for c in range(OC):
                P.wait("scalar", vV, y2v)
                if c >= 2:
                    P.wait("scalar", tT, y2mm[c - 2])
                P.run("scalar", lambda e, c=c: e.activation(
                    y2sq[c % 2], X1[:, TB * c:TB * (c + 1)], AF.Square, bias=0.0, scale=1.0),
                    inc=sS)
                P.wait("tensor", sS)
                if c == 0:
                    bank_wait(2)
                    bank_wait(3)
                P.run("tensor", lambda e, c=c: e.matmul(
                    ps[2][0:1, 0:TB], ONEC, X1[:, TB * c:TB * (c + 1)],
                    start=(c == 0), stop=(c == OC - 1)))
                P.run("tensor", lambda e, c=c: e.matmul(
                    ps[3][0:1, 0:TB], ONEC, y2sq[c % 2],
                    start=(c == 0), stop=(c == OC - 1)), inc=tT)
                y2mm[c] = tT.n
            P.wait("scalar", tT)
            P.run("scalar", lambda e: e.activation(RM, ps[2][0:1, 0:TB], AF.Identity,
                                                   bias=0.0, scale=1.0 / D))
            P.run("scalar", lambda e: e.activation(RMSQ, ps[3][0:1, 0:TB], AF.Identity,
                                                   bias=0.0, scale=1.0 / D), inc=sS)
            bank_read(2, sS)
            bank_read(3, sS)
            P.wait("vector", sS)
            P.run("vector", lambda e: e.tensor_tensor(RVAR, RM, RM, ALU.mult))
            P.run("vector", lambda e: e.tensor_tensor(RVAR, RMSQ, RVAR, ALU.subtract), inc=vV)
            P.wait("scalar", vV)
            P.run("scalar", lambda e: e.activation(RSTD, RVAR, AF.Sqrt, bias=EPS, scale=1.0),
                  inc=sS)
            P.wait("vector", sS)
            P.run("vector", lambda e: e.reciprocal(RSTDr, RSTD))
            P.run("vector", lambda e: e.tensor_tensor(RB1, RM, RSTDr, ALU.mult), inc=vV)
            bcast_row(RSTDr, F1B, 4)
            bcast_row(RB1, B1B, 5)
            ost = [arB[:, 0:TB], arB[:, 512:512 + TB]]
            ost_st = {}
            P.wait("vector", sS)
            for c in range(KC):
                b = c % 2
                if c >= 2:
                    P.wait("vector", dOS[b])
                P.run("vector", lambda e, c=c, b=b: e.tensor_tensor(
                    ost[b], X1[:, TB * c:TB * (c + 1)], F1B[:, 0:TB], ALU.mult))
                P.run("vector", lambda e, c=c, b=b: e.tensor_tensor(
                    ost[b], ost[b], B1B[:, 0:TB], ALU.subtract), inc=vV)
                P.wait("sync", vV)
                P.run("sync", lambda e, c=c, b=b, tb=tb: e.dma_start(
                    out=OUT[c][:, TB * tb:TB * (tb + 1)], in_=ost[b]), inc=dOS[b], inc_n=16)
                ost_st[c] = dOS[b].n

        if debug:
            P.wait("sync", vV)
            P.run("sync", lambda e: e.dma_start(out=DBGY, in_=X1[:, 0:8192]), inc=dDB, inc_n=16)
            P.run("sync", lambda e: e.dma_start(out=DBGB[:, 0:512], in_=arD[:, 2 * TB:3 * TB]),
                  inc=dDB, inc_n=16)
            P.run("sync", lambda e: e.dma_start(out=DBGB[:, 512:1024], in_=arD[:, 3 * TB:4 * TB]),
                  inc=dDB, inc_n=16)
            P.run("sync", lambda e: e.dma_start(out=DBG[:, 0:16], in_=Acol), inc=dDB, inc_n=16)
            P.run("sync", lambda e: e.dma_start(out=DBG[:, 16:19], in_=sml[:, 104:107],
                                                allow_slow_non_contiguous=True),
                  inc=dDB, inc_n=16)
        for sem in (dQS, dTS[0], dTS[1], dOS[0], dOS[1], dDB):
            P.wait("sync", sem)

        for eng in _Prog.ENGS:
            getattr(block, eng)(lambda e, q=tuple(P.q[eng]): [f(e) for f in q])
    return nc


_NC_CACHE = {}


def kernel(**inputs):
    per_core = build_host(inputs)
    if "nc" not in _NC_CACHE:
        _NC_CACHE["nc"] = build_nc()
    nc = _NC_CACHE["nc"]
    res = run_bass_kernel_spmd(nc, per_core, core_ids=list(range(NCORES)))
    out = np.empty((NCORES, S, D), np.float32)
    for b in range(NCORES):
        o = np.asarray(res.results[b]["OUT"])
        out[b] = o.reshape(D, S).T
    return out
